# version 11
# baseline (speedup 1.0000x reference)
import numpy as np
import ml_dtypes
from contextlib import ExitStack

import concourse.bass as bass
import concourse.mybir as mybir
from concourse.bass_utils import run_bass_kernel_spmd

F32 = mybir.dt.float32
BF16 = mybir.dt.bfloat16
I8 = mybir.dt.int8
AF = mybir.ActivationFunctionType
ALU = mybir.AluOpType
AXX = mybir.AxisListType.X

BF = ml_dtypes.bfloat16

E = 2048
T = 1024
HALO = 3
NH = 64
CS = 256
NCH = T // CS
PROJ = 8512
PC = 67
GATE_C = 32
HBC_C = 34
INTER = 4096
IC = 32
FI = 8192
FIC = 64
DHC = 16
EC = 16
EPS = 1e-5


class Op:
    __slots__ = ("eng", "fn", "waits", "incs")
    def __init__(self, eng, fn, waits, incs):
        self.eng, self.fn, self.waits, self.incs = eng, fn, waits, incs


class Prog:
    ENGINES = ("sync", "tensor", "vector", "scalar", "gpsimd")

    def __init__(self, nc, es):
        self.nc = nc
        self.ops = []
        self.sems = {}
        self.cnt = {}
        self.es = es

    def sem(self, name):
        if name not in self.sems:
            self.sems[name] = self.es.enter_context(self.nc.semaphore(name))
            self.cnt[name] = 0
        return self.sems[name]

    def val(self, name):
        self.sem(name)
        return self.cnt.get(name, 0)

    def op(self, eng, fn, waits=(), inc=None, n=1):
        incs = []
        if inc is not None:
            self.sem(inc)
            self.cnt[inc] += n
            incs = [(inc, n)]
        ws = []
        for s, v in waits:
            self.sem(s)
            if v > 0:
                ws.append((s, v))
        self.ops.append(Op(eng, fn, ws, incs))

    def barrier(self):
        self.sem("PH")
        for e in self.ENGINES:
            self.cnt["PH"] += 1
            fn = None if e == "sync" else "DRAIN"
            self.ops.append(Op(e, fn, [], [("PH", 1)]))
        v = self.cnt["PH"]
        for e in self.ENGINES:
            self.ops.append(Op(e, "WAITONLY", [("PH", v)], []))

    def replay(self, engname, e):
        for o in self.ops:
            if o.eng != engname:
                continue
            for s, v in o.waits:
                e.wait_ge(self.sems[s], v)
            if o.fn == "WAITONLY":
                continue
            if o.fn == "DRAIN":
                inst = e.drain()
            else:
                inst = e.nop() if o.fn is None else o.fn(e)
            for s, n in o.incs:
                if inst is None:
                    inst = e.nop()
                inst.then_inc(self.sems[s], n)

    def emit(self, block):
        block.sync(lambda e: self.replay("sync", e))
        block.tensor(lambda e: self.replay("tensor", e))
        block.vector(lambda e: self.replay("vector", e))
        block.scalar(lambda e: self.replay("scalar", e))
        block.gpsimd(lambda e: self.replay("gpsimd", e))


def build_nc(debug_taps=()):
    nc = bass.Bass("TRN2", target_bir_lowering=False, debug=False)

    dx = nc.dram_tensor("x", [T, E], F32, kind="ExternalInput").ap()
    dxh = nc.dram_tensor("xh", [HALO, E], F32, kind="ExternalInput").ap()
    d_win = nc.dram_tensor("w_in_img", [PC, 16, 128, 128], F32, kind="ExternalInput").ap()
    d_wout = nc.dram_tensor("w_out_img", [DHC, 32, 128, 128], F32, kind="ExternalInput").ap()
    d_wg = nc.dram_tensor("wg_img", [FIC, 16, 128, 128], I8, kind="ExternalInput").ap()
    d_wu = nc.dram_tensor("wu_img", [FIC, 16, 128, 128], I8, kind="ExternalInput").ap()
    d_wd = nc.dram_tensor("wd_img", [DHC, 64, 128, 128], I8, kind="ExternalInput").ap()
    d_convw = nc.dram_tensor("convw", [128, HBC_C * 4], F32, kind="ExternalInput").ap()
    d_convb = nc.dram_tensor("convb", [128, HBC_C], F32, kind="ExternalInput").ap()
    d_normw = nc.dram_tensor("normw", [128, IC], F32, kind="ExternalInput").ap()
    d_sg = nc.dram_tensor("sgate", [128, FIC], F32, kind="ExternalInput").ap()
    d_su = nc.dram_tensor("sup", [128, FIC], F32, kind="ExternalInput").ap()
    d_sd = nc.dram_tensor("sdown", [128, DHC], F32, kind="ExternalInput").ap()
    d_dtb = nc.dram_tensor("dtb", [NH, 1], F32, kind="ExternalInput").ap()
    d_alog = nc.dram_tensor("alog", [NH, 1], F32, kind="ExternalInput").ap()
    d_dpair = nc.dram_tensor("dpair", [128, IC], F32, kind="ExternalInput").ap()
    d_idf = nc.dram_tensor("idf", [128, 128], F32, kind="ExternalInput").ap()
    d_idb = nc.dram_tensor("idb", [128, 128], BF16, kind="ExternalInput").ap()
    d_gmask = nc.dram_tensor("gmask", [128, 512], BF16, kind="ExternalInput").ap()
    d_fc = nc.dram_tensor("fconst", [128, 2], F32, kind="ExternalInput").ap()
    d_out = nc.dram_tensor("out", [T, E], F32, kind="ExternalOutput").ap()

    taps = {}
    def reg_tap(name, shape, dt):
        taps[name] = nc.dram_tensor("tap_" + name, shape, dt, kind="ExternalOutput").ap()

    with ExitStack() as es:
        P = Prog(nc, es)

        def sb(name, shape, dt=F32):
            return es.enter_context(nc.sbuf_tensor(name, shape, dt))

        xn = sb("xn", [128, EC * 1024], BF16)
        xnh = sb("xnh", [128, EC * HALO], BF16)
        bufA = sb("bufA", [128, GATE_C * 1024], BF16)
        bufB = sb("bufB", [128, HBC_C * 1027], BF16)
        dt_raw = sb("dt_raw", [NH, T], F32)
        convw = sb("convw_s", [128, HBC_C * 4], F32)
        convb = sb("convb_s", [128, HBC_C], F32)
        normw = sb("normw_s", [128, IC], F32)
        sgate = sb("sgate_s", [128, FIC], F32)
        sup = sb("sup_s", [128, FIC], F32)
        sdown = sb("sdown_s", [128, DHC], F32)
        dtb = sb("dtb_s", [NH, 1], F32)
        alog = sb("alog_s", [NH, 1], F32)
        dpair = sb("dpair_s", [128, IC], F32)
        idf = sb("idf_s", [128, 128], F32)
        idb = sb("idb_s", [128, 128], BF16)
        gmask = sb("gmask_s", [128, 512], BF16)
        negA = sb("negA", [NH, 1], F32)
        expa = sb("expa", [NH, 1], F32)
        ones1 = sb("ones1", [1, 128], F32)
        onesb = sb("onesb", [128, 1], BF16)
        zer_c = sb("zer_c", [NH, CS], F32)
        fcst = sb("fcst", [128, 2], F32)

        def xn_sl(k, c0, w):
            return xn[:, 1024 * k + c0: 1024 * k + c0 + w]
        def xnh_sl(k):
            return xnh[:, HALO * k: HALO * k + HALO]
        def hbc_sl(cc, c0, w):
            return bufB[:, 1027 * cc + c0: 1027 * cc + c0 + w]
        def gate_sl(j, c0, w):
            return bufA[:, 1024 * j + c0: 1024 * j + c0 + w]

        for dst, src in [(convw, d_convw), (convb, d_convb), (normw, d_normw),
                         (sgate, d_sg), (sup, d_su), (sdown, d_sd),
                         (dtb, d_dtb), (alog, d_alog), (dpair, d_dpair),
                         (idf, d_idf), (idb, d_idb), (gmask, d_gmask),
                         (fcst, d_fc)]:
            P.op("sync", lambda e, d=dst, s=src: e.dma_start(out=d[:], in_=s),
                 inc="DC", n=16)
        dcv = P.val("DC")
        P.op("vector", lambda e: e.memset(ones1[:], 1.0))
        P.op("vector", lambda e: e.memset(onesb[:], 1.0))
        P.op("vector", lambda e: e.memset(zer_c[:], 0.0))
        P.op("scalar", lambda e: e.activation(expa[:], alog[:], AF.Exp),
             waits=[("DC", dcv)], inc="INI")
        P.op("vector", lambda e: e.tensor_scalar_mul(negA[:], expa[:], -1.0),
             waits=[("INI", 1)], inc="INI")

        with ExitStack() as ph:
            def sbp(name, shape, dt=F32):
                return ph.enter_context(nc.sbuf_tensor(name, shape, dt))
            xt = [sbp(f"xt{i}", [128, E], F32) for i in range(2)]
            xht = sbp("xht", [128, E], F32)
            sqt = sbp("sqt", [128, E], BF16)
            s1 = sbp("s1", [128, 1], F32)
            ssq = sbp("ssq", [128, 1], F32)
            mcol = sbp("mcol", [128, 1], F32)
            mm2 = sbp("mm2", [128, 1], F32)
            vcol = sbp("vcol", [128, 1], F32)
            rstd = sbp("rstd", [128, 1], F32)
            mb = sbp("mb", [128, 1], F32)
            xn_tm = sbp("xn_tm", [128, E], BF16)
            stt_dump = sbp("stt_dump", [128, 7], F32)
            p_tr = ph.enter_context(nc.psum_tensor("p_tr", [128, 1024], BF16))

            def ln_tile(i, is_halo):
                buf = xht if is_halo else xt[i % 2]
                src = dxh if is_halo else dx[128 * i:128 * i + 128, :]
                npart = 128
                bb = buf[:]
                w = [("LNT", 2 * (i - 1))] if (not is_halo and i >= 2) else []
                if is_halo:
                    P.op("vector", lambda e: e.memset(xht[:], 0.0), inc="LNH")
                    P.op("sync", lambda e: e.dma_start(out=xht[0:HALO, :], in_=src),
                         waits=[("LNH", 1)], inc="DX", n=16)
                else:
                    P.op("sync", lambda e: e.dma_start(out=bb, in_=src),
                         waits=w, inc="DX", n=16)
                ldv = P.val("DX")
                w = [("DX", ldv)]
                if not is_halo and i == 0:
                    w.append(("DC", P.val("DC")))
                if not is_halo and i >= 1:
                    w.append(("LNV", 5 * (i - 1) + 4))
                elif is_halo:
                    w.append(("LNV", 5 * 7 + 4))
                P.op("scalar", lambda e: e.activation(
                    sqt[0:npart, :], bb, AF.Square, accum_out=ssq[0:npart, :]),
                    waits=w, inc="LNS")
                P.op("vector", lambda e: e.reduce_sum(s1[0:npart, :], bb, axis=AXX),
                     waits=[("DX", ldv)], inc="LNV")
                sv = P.val("LNS")
                rdv = P.val("LNV")
                P.op("vector", lambda e: e.tensor_scalar_mul(
                    mcol[0:npart, :], s1[0:npart, :], 1.0 / E),
                    waits=[("LNV", rdv)], inc="LNV")
                rdv = P.val("LNV")
                P.op("vector", lambda e: e.tensor_mul(
                    mm2[0:npart, :], mcol[0:npart, :], mcol[0:npart, :]),
                    waits=[("LNV", rdv)], inc="LNV")
                rdv = P.val("LNV")
                P.op("vector", lambda e: e.scalar_tensor_tensor(
                    vcol[0:npart, :], ssq[0:npart, :], 1.0 / E, mm2[0:npart, :],
                    ALU.mult, ALU.subtract),
                    waits=[("LNS", sv), ("LNV", rdv)], inc="LNV")
                vv = P.val("LNV")
                P.op("scalar", lambda e: e.activation(
                    rstd[0:npart, :], vcol[0:npart, :], AF.Ln,
                    bias=fcst[0:npart, 0:1]),
                    waits=[("LNV", vv)], inc="LNS")
                lnv_ = P.val("LNS")
                P.op("scalar", lambda e: e.activation(
                    rstd[0:npart, :], rstd[0:npart, :], AF.Exp, scale=-0.5),
                    waits=[("LNS", lnv_)], inc="LNS")
                sv = P.val("LNS")
                P.op("vector", lambda e: e.scalar_tensor_tensor(
                    mb[0:npart, :], mcol[0:npart, :], -1.0, rstd[0:npart, :],
                    ALU.mult, ALU.mult),
                    waits=[("LNS", sv)], inc="LNV")
                vv = P.val("LNV")
                w = [("LNV", vv)]
                pv_prev = P.val("LNP")
                if pv_prev > 0:
                    w.append(("LNP", pv_prev))
                P.op("scalar", lambda e: e.activation(
                    xn_tm[0:npart, :], bb, AF.Identity, bias=mb[0:npart, :],
                    scale=rstd[0:npart, :]),
                    waits=w, inc="LNS")
                sv = P.val("LNS")
                if is_halo:
                    for half in range(2):
                        for k8 in range(8):
                            k = 8 * half + k8
                            w = []
                            if k == 0:
                                w.append(("LNS", sv))
                            if k8 == 0:
                                w.append(("LNT", P.val("LNT")))
                            P.op("tensor", lambda e, k=k, k8=k8: e.transpose(
                                p_tr[:, 128 * k8:128 * k8 + 128],
                                xn_tm[:, 128 * k:128 * k + 128], idb[:]),
                                waits=w, inc="LNP")
                        pv = P.val("LNP")
                        P.op("vector", lambda e, half=half: e.tensor_copy(
                            xnh[:, :].rearrange("p (k c) -> p k c", c=HALO)
                            [:, 8 * half:8 * half + 8, :],
                            p_tr[:, 0:1024].rearrange("p (k c) -> p k c", c=128)
                            [:, :, 0:HALO]),
                            waits=[("LNP", pv)], inc="LNT")
                else:
                    for half in range(2):
                        for k8 in range(8):
                            k = 8 * half + k8
                            w = []
                            if k == 0:
                                w.append(("LNS", sv))
                            if k8 == 0:
                                w.append(("LNT", P.val("LNT")))
                            P.op("tensor", lambda e, k=k, k8=k8: e.transpose(
                                p_tr[:, 128 * k8:128 * k8 + 128],
                                xn_tm[:, 128 * k:128 * k + 128], idb[:]),
                                waits=w, inc="LNP")
                        pv = P.val("LNP")
                        P.op("vector", lambda e, i=i, half=half: e.tensor_copy(
                            xn[:, :].rearrange("p (k t) -> p k t", t=1024)
                              [:, 8 * half:8 * half + 8, 128 * i:128 * i + 128],
                            p_tr[:, 0:1024].rearrange("p (k t) -> p k t", t=128)),
                            waits=[("LNP", pv)], inc="LNT")

            for i in range(8):
                ln_tile(i, False)
            ln_tile(0, True)
            P.barrier()
            if "lnh" in debug_taps:
                reg_tap("lnh_xt", [HALO, E], F32)
                reg_tap("lnh_xntm", [HALO, E], BF16)
                reg_tap("lnh_st", [128, 7], F32)
                for ci, st in enumerate([s1, ssq, mcol, mm2, vcol, rstd, mb]):
                    P.op("vector", lambda e, ci=ci, st=st: e.tensor_copy(
                        stt_dump[:, ci:ci + 1], st[:]), inc="TAP0")
                P.op("sync", lambda e: e.dma_start(out=taps["lnh_xt"], in_=xht[0:HALO, :]),
                     waits=[("TAP0", 7)], inc="TAP", n=16)
                P.op("sync", lambda e: e.dma_start(out=taps["lnh_xntm"],
                     in_=xn_tm[0:HALO, :]), inc="TAP", n=16)
                P.op("sync", lambda e: e.dma_start(out=taps["lnh_st"],
                     in_=stt_dump[:]), inc="TAP", n=16)
                P.barrier()
            if "xn" in debug_taps:
                reg_tap("xn", [128, EC * 1024], BF16)
                reg_tap("xnh", [128, EC * HALO], BF16)
                P.op("sync", lambda e: e.dma_start(out=taps["xn"], in_=xn[:]),
                     inc="TAP", n=16)
                P.op("sync", lambda e: e.dma_start(out=taps["xnh"], in_=xnh[:]),
                     inc="TAP", n=16)
                P.op("sync", "WAITONLY", waits=[("TAP", P.val("TAP"))])
                P.barrier()

        with ExitStack() as ph:
            wf = [ph.enter_context(nc.sbuf_tensor(f"wf{i}", [128, 2048], F32))
                  for i in range(2)]
            wb = [ph.enter_context(nc.sbuf_tensor(f"wb{i}", [128, 2048], BF16))
                  for i in range(2)]
            pps = [ph.enter_context(nc.psum_tensor(f"pps{i}", [128, 512], F32))
                   for i in range(4)]
            acc = [ph.enter_context(nc.sbuf_tensor(f"acc{i}", [128, T], F32))
                   for i in range(2)]

            def emit_conv(cc):
                a0, a1 = acc[0], acc[1]
                w = [("PEV", 3 * (cc + 1))]
                if cc >= 1:
                    w.append(("CVS", cc))
                P.op("vector", lambda e, cc=cc, a0=a0: e.tensor_scalar(
                    a0[:], hbc_sl(cc, 0, T), convw[:, 4 * cc:4 * cc + 1],
                    convb[:, cc:cc + 1], ALU.mult, ALU.add),
                    waits=w, inc="CV0")
                v0 = P.val("CV0")
                P.op("vector", lambda e, cc=cc, a0=a0, a1=a1: e.scalar_tensor_tensor(
                    a1[:], hbc_sl(cc, 1, T), convw[:, 4 * cc + 1:4 * cc + 2], a0[:],
                    ALU.mult, ALU.add),
                    waits=[("CV0", v0)], inc="CV1")
                v1 = P.val("CV1")
                P.op("vector", lambda e, cc=cc, a0=a0, a1=a1: e.scalar_tensor_tensor(
                    a0[:], hbc_sl(cc, 2, T), convw[:, 4 * cc + 2:4 * cc + 3], a1[:],
                    ALU.mult, ALU.add),
                    waits=[("CV1", v1)], inc="CV2")
                v2 = P.val("CV2")
                P.op("vector", lambda e, cc=cc, a0=a0, a1=a1: e.scalar_tensor_tensor(
                    a1[:], hbc_sl(cc, 3, T), convw[:, 4 * cc + 3:4 * cc + 4], a0[:],
                    ALU.mult, ALU.add),
                    waits=[("CV2", v2)], inc="CV3")
                v3 = P.val("CV3")
                P.op("scalar", lambda e, cc=cc, a1=a1: e.activation(
                    hbc_sl(cc, HALO, T), a1[:], AF.Silu),
                    waits=[("CV3", v3)], inc="CVS")

            order = list(range(32, 67)) + list(range(32))
            cumg = [0]
            for fc_ in order:
                cumg.append(cumg[-1] + (3 if 32 <= fc_ <= 65 else 2))
            grp = 0
            for oi, fc in enumerate(order):
                w = [("PCA", oi - 1)] if oi >= 2 else []
                P.op("sync", lambda e, fc=fc, b=oi % 2: e.dma_start(
                    out=wf[b][:], in_=d_win[fc].rearrange("k p m -> p k m")),
                    waits=w, inc="PDW", n=16)
                w = [("PDW", 16 * (oi + 1))]
                if oi >= 2:
                    w.append(("PMQ", cumg[oi - 1]))
                P.op("scalar", lambda e, b=oi % 2: e.copy(wb[b][:], wf[b][:]),
                     waits=w, inc="PCA")
                cav = P.val("PCA")
                groups = [(HALO, 512), (HALO + 512, 512)]
                if 32 <= fc <= 65:
                    groups.append((0, HALO))
                for gi, (c0, wdt) in enumerate(groups):
                    slot = grp % 4
                    wts = [("PCA", cav)] if gi == 0 else []
                    if grp >= 4:
                        wts.append(("PEV", grp - 3))
                    def mmg(e, b=oi % 2, c0=c0, wdt=wdt, slot=slot):
                        last = None
                        for k in range(EC):
                            rhs = xnh_sl(k) if c0 == 0 else xn_sl(k, c0 - HALO, wdt)
                            last = e.matmul(pps[slot][:, 0:wdt],
                                            wb[b][:, 128 * k:128 * k + 128],
                                            rhs, start=(k == 0), stop=(k == EC - 1))
                        return last
                    P.op("tensor", mmg, waits=wts, inc="PMQ")
                    grp += 1
                    mqv = P.val("PMQ")
                    if fc < GATE_C:
                        P.op("scalar", lambda e, fc=fc, c0=c0, wdt=wdt, slot=slot:
                             e.activation(gate_sl(fc, c0 - HALO, wdt),
                                          pps[slot][:, 0:wdt], AF.Silu),
                             waits=[("PMQ", mqv)], inc="PEV")
                    elif fc < 66:
                        P.op("vector", lambda e, cc=fc - 32, c0=c0, wdt=wdt, slot=slot:
                             e.tensor_copy(hbc_sl(cc, c0, wdt), pps[slot][:, 0:wdt]),
                             waits=[("PMQ", mqv)], inc="PEV")
                    else:
                        P.op("vector", lambda e, c0=c0, wdt=wdt, slot=slot:
                             e.tensor_copy(dt_raw[:, c0 - HALO:c0 - HALO + wdt],
                                           pps[slot][0:NH, 0:wdt]),
                             waits=[("PMQ", mqv)], inc="PEV")
                if 1 <= oi <= HBC_C:
                    emit_conv(oi - 1)
            P.barrier()
            if "hbc_raw" in debug_taps:
                reg_tap("hbc_raw", [128, HBC_C * 1027], BF16)
                P.op("sync", lambda e: e.dma_start(out=taps["hbc_raw"], in_=bufB[:]),
                     inc="TAP", n=16)
            if "gate" in debug_taps:
                reg_tap("gate", [128, GATE_C * 1024], BF16)
                P.op("sync", lambda e: e.dma_start(out=taps["gate"], in_=bufA[:]),
                     inc="TAP", n=16)
            if "dt_raw" in debug_taps:
                reg_tap("dt_raw", [NH, T], F32)
                P.op("sync", lambda e: e.dma_start(out=taps["dt_raw"], in_=dt_raw[:]),
                     inc="TAP", n=16)
            if debug_taps:
                P.op("sync", "WAITONLY", waits=[("TAP", P.val("TAP"))])
                P.barrier()

        with ExitStack() as ph:
            def sbp(name, shape, dt=F32):
                return ph.enter_context(nc.sbuf_tensor(name, shape, dt))
            htm = [sbp(f"htm{s}", [128, INTER], BF16) for s in range(2)]
            acumT = sbp("acumT", [128, 2 * 64], F32)
            dtT = sbp("dtT", [128, 2 * 64], F32)
            ddT = sbp("ddT", [128, 2 * 64], F32)
            btm = sbp("btm", [128, 2 * 128], BF16)
            dt_c = sbp("dt_c", [NH, CS], F32)
            ac_c = sbp("ac_c", [NH, CS], F32)
            sp3 = sbp("sp3", [NH, CS], F32)
            Esb = [sbp(f"Esb{i}", [128, 512], BF16) for i in range(2)]
            MTsb = [sbp(f"MTsb{i}", [128, 512], BF16) for i in range(2)]
            Mtmp = sbp("Mtmp", [128, 512], BF16)
            Gsb = sbp("Gsb", [128, 512], BF16)
            sdr = sbp("sdr", [128, CS], BF16)
            st_sb = sbp("st_sb", [128, 128], BF16)
            hdd = sbp("hdd", [128, 4 * 64], BF16)
            u_a = sbp("u_a", [128, CS], F32)
            u_b = sbp("u_b", [128, CS], F32)
            sp1 = u_a[0:NH, :]
            sp2 = u_b[0:NH, :]
            dec_c = sp1
            dd_c = sp2
            rows = [sbp(f"rows{i}", [1, CS], F32) for i in range(3)]
            p_rep = [ph.enter_context(nc.psum_tensor(f"p_rep{i}", [128, 512], F32))
                     for i in range(2)]
            p_yd = [ph.enter_context(nc.psum_tensor(f"p_yd{i}", [128, 512], F32))
                    for i in range(2)]
            p_yo = ph.enter_context(nc.psum_tensor("p_yo", [128, 512], F32))
            p_st = ph.enter_context(nc.psum_tensor("p_st", [128, 512], F32))
            p_tb = ph.enter_context(nc.psum_tensor("p_tb", [128, 1024], BF16))
            p_tf = ph.enter_context(nc.psum_tensor("p_tf", [128, 512], F32))

            for c in range(NCH):
                c0 = HALO + CS * c
                P.op("vector", lambda e, c=c: e.tensor_scalar_add(
                    sp1, dt_raw[:, CS * c:CS * c + CS], dtb[:]), inc="CP")
                v = P.val("CP")
                P.op("vector", lambda e: e.scalar_tensor_tensor(
                    sp2, sp1, -1.0, sp1, ALU.mult, ALU.max),
                    waits=[("CP", v)], inc="CP")
                v = P.val("CP")
                P.op("scalar", lambda e: e.activation(sp3[:], sp2, AF.Exp,
                                                      scale=-1.0),
                     waits=[("CP", v)], inc="CP")
                v = P.val("CP")
                P.op("scalar", lambda e: e.activation(sp2, sp3[:], AF.Ln,
                                                      bias=fcst[0:NH, 1:2]),
                     waits=[("CP", v)], inc="CP")
                v = P.val("CP")
                P.op("vector", lambda e: e.tensor_scalar_max(sp3[:], sp1, 0.0),
                     waits=[("CP", v)], inc="CP")
                v = P.val("CP")
                P.op("vector", lambda e: e.tensor_add(dt_c[:], sp3[:], sp2),
                     waits=[("CP", v)], inc="CP")
                v = P.val("CP")
                P.op("vector", lambda e: e.tensor_scalar_mul(sp1, dt_c[:],
                                                             negA[:]),
                     waits=[("CP", v)], inc="CP")
                v = P.val("CP")
                P.op("vector", lambda e: e.tensor_tensor_scan(
                    ac_c[:], sp1, zer_c[:], 0.0, ALU.add, ALU.add),
                    waits=[("CP", v)], inc="CP")
                v = P.val("CP")
                P.op("scalar", lambda e: e.activation(
                    dec_c, ac_c[:], AF.Exp, bias=ac_c[:, CS - 1:CS],
                    scale=-1.0), waits=[("CP", v)], inc="CP")
                v = P.val("CP")
                P.op("vector", lambda e: e.tensor_mul(dd_c, dt_c[:], dec_c),
                     waits=[("CP", v)], inc="CP")
                v = P.val("CP")
                for s in range(2):
                    P.op("tensor", lambda e, s=s: e.transpose(
                        p_tf[:, 64 * s:64 * s + 64],
                        ac_c[:, 128 * s:128 * s + 128], idf[0:NH, 0:NH]),
                        waits=[("CP", v)] if s == 0 else [], inc="CPT")
                    P.op("tensor", lambda e, s=s: e.transpose(
                        p_tf[:, 128 + 64 * s:128 + 64 * s + 64],
                        dt_c[:, 128 * s:128 * s + 128], idf[0:NH, 0:NH]),
                        inc="CPT")
                    P.op("tensor", lambda e, s=s: e.transpose(
                        p_tf[:, 256 + 64 * s:256 + 64 * s + 64],
                        dd_c[:, 128 * s:128 * s + 128], idf[0:NH, 0:NH]),
                        inc="CPT")
                tv = P.val("CPT")
                P.op("vector", lambda e: e.tensor_scalar_mul(
                    acumT[:], p_tf[:, 0:128], -1.0),
                     waits=[("CPT", tv)], inc="CP")
                P.op("vector", lambda e: e.tensor_copy(dtT[:], p_tf[:, 128:256]),
                     inc="CP")
                P.op("vector", lambda e: e.tensor_copy(ddT[:], p_tf[:, 256:384]),
                     inc="CP")
                cpv = P.val("CP")
                for s in range(2):
                    P.op("tensor", lambda e, c0=c0, s=s: e.transpose(
                        p_tb[:, 128 * s:128 * s + 128],
                        bufB[:, 1027 * 32 + c0 + 128 * s:
                             1027 * 32 + c0 + 128 * s + 128], idb[:]),
                        waits=[("CP", cpv)] if s == 0 else [], inc="CPT")
                tv = P.val("CPT")
                P.op("vector", lambda e: e.tensor_copy(btm[:], p_tb[:, 0:256]),
                     waits=[("CPT", tv)], inc="CP")
                gv = P.val("CP")
                def gmm(e, c0=c0):
                    last = None
                    for s in range(2):
                        last = e.matmul(
                            p_tf[:, 256 * s:256 * s + 256],
                            bufB[:, 1027 * 32 + c0 + 128 * s:
                                 1027 * 32 + c0 + 128 * s + 128],
                            bufB[:, 1027 * 33 + c0:1027 * 33 + c0 + CS],
                            start=True, stop=True, skip_group_check=True)
                    return last
                P.op("tensor", gmm, waits=[("CP", gv)], inc="CPT")
                tv = P.val("CPT")
                P.op("vector", lambda e: e.tensor_tensor(
                    Gsb[:], p_tf[:, 0:512], gmask[:], ALU.mult),
                    waits=[("CPT", tv)], inc="CP")
                for s in range(2):
                    for fc8 in range(4):
                        for k8 in range(8):
                            fc = 8 * fc8 + k8
                            w = []
                            if s == 0 and fc8 == 0 and k8 == 0:
                                w.append(("CP", P.val("CP")))
                            if k8 == 0:
                                w.append(("CPH", P.val("CPH")))
                            P.op("tensor", lambda e, c0=c0, s=s, fc=fc, k8=k8:
                                 e.transpose(
                                     p_tb[:, 128 * k8:128 * k8 + 128],
                                     bufB[:, 1027 * fc + c0 + 128 * s:
                                          1027 * fc + c0 + 128 * s + 128],
                                     idb[:]),
                                 waits=w, inc="CPT")
                        tv = P.val("CPT")
                        P.op("vector", lambda e, s=s, fc8=fc8: e.tensor_copy(
                            htm[s][:, 1024 * fc8:1024 * fc8 + 1024],
                            p_tb[:, 0:1024]),
                            waits=[("CPT", tv)], inc="CPH")
                cpv = P.val("CP")
                cphv = P.val("CPH")

                q0 = c * IC * 2

                def emit_rows(q, j2, h2):
                    hh = 2 * j2 + h2
                    w = [("CP", cpv)]
                    if q >= 3:
                        w.append(("RL", q - 2))
                    P.op("sync", lambda e, hh=hh, b=q % 3: e.dma_start(
                        out=rows[b][0:1, :], in_=ac_c[hh:hh + 1, :]),
                        waits=w, inc="R16", n=16)

                def emit_repl(q):
                    w = [("R16", 16 * (q - q0 + 1) + r16base)]
                    if q >= 2:
                        w.append(("EXD", 2 * (q - 1)))
                        w.append(("SDR", q - 1))
                    P.op("tensor", lambda e, b=q % 2, rb=q % 3: e.matmul(
                        p_rep[b][:, 0:256], ones1[:], rows[rb][0:1, :],
                        start=True, stop=True, skip_group_check=True),
                        waits=w, inc="RL")

                r16base = P.val("R16")
                rlbase = P.val("RL")
                emit_rows(q0, 0, 0)
                emit_rows(q0 + 1, 0, 1)
                emit_repl(q0)
                for j in range(IC):
                    pi = c * IC + j
                    for h in range(2):
                        q = 2 * pi + h
                        hh = 2 * j + h
                        par = q % 2
                        if q + 1 < q0 + 2 * IC:
                            if q + 2 < q0 + 2 * IC:
                                j2, h2 = divmod(q + 2 - q0, 2)
                                emit_rows(q + 2, (q + 2 - q0) // 2,
                                          (q + 2 - q0) % 2)
                            emit_repl(q + 1)
                        rl = rlbase + (q - q0) + 1
                        for s in range(2):
                            w = [("RL", rl)] if s == 0 else []
                            if q >= 2 and s == 0:
                                w.append(("MDT", 2 * (q - 1)))
                            P.op("scalar", lambda e, s=s, hh=hh, b=par:
                                 e.activation(
                                     Esb[b][:, 256 * s:256 * s + 256],
                                     p_rep[b][:, 0:256], AF.Exp,
                                     bias=acumT[:, 64 * s + hh:64 * s + hh + 1]),
                                 waits=w, inc="EXD")
                        exd = P.val("EXD")
                        for s in range(2):
                            w = [("EXD", exd)] if s == 0 else []
                            if q >= 1:
                                w.append(("GMD", 2 * (q - 1) + s + 1))
                            P.op("vector", lambda e, s=s, hh=hh, b=par:
                                 e.tensor_scalar(
                                     Mtmp[:, 256 * s:256 * s + 256],
                                     Esb[b][:, 256 * s:256 * s + 256],
                                     1.0,
                                     dtT[:, 64 * s + hh:64 * s + hh + 1],
                                     ALU.min, ALU.mult),
                                 waits=w, inc="MDT")
                            w2 = []
                            if q >= 2 and s == 0:
                                w2.append(("YDD", 2 * (q - 1)))
                            P.op("vector", lambda e, s=s, b=par:
                                 e.tensor_tensor(
                                     MTsb[b][:, 256 * s:256 * s + 256],
                                     Mtmp[:, 256 * s:256 * s + 256],
                                     Gsb[:, 256 * s:256 * s + 256], ALU.mult),
                                 waits=w2, inc="GMD")
                        gmd = P.val("GMD")
                        w = [("CPH", cphv)]
                        if pi >= 1:
                            if h == 0:
                                w.append(("STD", 4 * pi))
                        for s in range(2):
                            seg = 2 * h + s
                            P.op("vector", lambda e, seg=seg, s=s, hh=hh:
                                 e.tensor_scalar_mul(
                                     hdd[:, 64 * seg:64 * seg + 64],
                                     htm[s][:, 64 * hh:64 * hh + 64],
                                     ddT[:, 64 * s + hh:64 * s + hh + 1]),
                                 waits=w if s == 0 else [], inc="HTD")
                        htd = P.val("HTD")
                        w = [("GMD", gmd), ("CPH", cphv)]
                        if pi >= 2 and h == 0:
                            w.append(("U2D", pi - 1))
                        def ydmm(e, h=h, hh=hh, b=par, bp=pi % 2):
                            last = None
                            for s in range(2):
                                last = e.matmul(
                                    p_yd[bp][64 * h:64 * h + 64, 0:256],
                                    htm[s][:, 64 * hh:64 * hh + 64],
                                    MTsb[b][:, 256 * s:256 * s + 256],
                                    start=(s == 0), stop=(s == 1),
                                    skip_group_check=True,
                                    tile_position=(0, 64 * h))
                            return last
                        P.op("tensor", ydmm, waits=w, inc="YDD", n=2)
                    for h in range(2):
                        q = 2 * pi + h
                        w = [("RL", rlbase + (q - q0) + 1)]
                        if pi >= 1 and h == 0:
                            w.append(("U1D", pi))
                        P.op("scalar", lambda e, h=h, b=q % 2: e.activation(
                            sdr[64 * h:64 * h + 64, :],
                            p_rep[b][64 * h:64 * h + 64, 0:256],
                            AF.Exp), waits=w, inc="SDR")
                    sdrv = P.val("SDR")
                    w = [("HTD", P.val("HTD"))]
                    if pi >= 1:
                        w.append(("SEV", pi))
                    def stmm(e):
                        last = None
                        for h in range(2):
                            for s in range(2):
                                seg = 2 * h + s
                                last = e.matmul(
                                    p_st[:, 64 * h:64 * h + 64],
                                    btm[:, 128 * s:128 * s + 128],
                                    hdd[:, 64 * seg:64 * seg + 64],
                                    start=(s == 0), stop=(s == 1),
                                    skip_group_check=True)
                        return last
                    P.op("tensor", stmm, waits=w, inc="STD", n=4)
                    stv = P.val("STD")
                    w = [("STD", stv)]
                    if pi >= 1:
                        w.append(("YOD", 2 * pi))
                    P.op("vector", lambda e: e.tensor_copy(st_sb[:], p_st[:, 0:128]),
                         waits=w, inc="SEV")
                    sev = P.val("SEV")
                    w = [("SEV", sev)]
                    if pi >= 1:
                        w.append(("U1D", pi))
                    def yomm(e, c0=c0):
                        last = None
                        for h in range(2):
                            last = e.matmul(
                                p_yo[64 * h:64 * h + 64, 0:256],
                                st_sb[:, 64 * h:64 * h + 64],
                                bufB[:, 1027 * 33 + c0:1027 * 33 + c0 + CS],
                                start=True, stop=True, skip_group_check=True,
                                tile_position=(0, 64 * h))
                        return last
                    P.op("tensor", yomm, waits=w, inc="YOD", n=2)
                    yod = P.val("YOD")
                    w = [("YOD", yod), ("SDR", sdrv)]
                    if pi >= 1:
                        w.append(("U4D", pi))
                    P.op("vector", lambda e: e.tensor_tensor(
                        u_a[:], p_yo[:, 0:256], sdr[:], ALU.mult),
                        waits=w, inc="U1D")
                    ydd = P.val("YDD")
                    w = [("YDD", ydd)]
                    if pi >= 1:
                        w.append(("U3D", pi))
                    P.op("vector", lambda e, bp=pi % 2: e.tensor_tensor(
                        u_b[:], u_a[:], p_yd[bp][:, 0:256], ALU.add),
                        waits=w, inc="U2D")
                    u2 = P.val("U2D")
                    w = [("U2D", u2)]
                    if pi >= 1:
                        w.append(("U4D", pi))
                    P.op("vector", lambda e, j=j, c0=c0: e.scalar_tensor_tensor(
                        u_a[:], bufB[:, 1027 * j + c0:1027 * j + c0 + CS],
                        dpair[:, j:j + 1], u_b[:], ALU.mult, ALU.add),
                        waits=w, inc="U3D")
                    u3 = P.val("U3D")
                    P.op("gpsimd", lambda e, j=j, c=c, c0=c0: e.tensor_tensor(
                        bufB[:, 1027 * j + c0:1027 * j + c0 + CS],
                        u_a[:], gate_sl(j, CS * c, CS), ALU.mult),
                        waits=[("U3D", u3)], inc="U4D")
                P.barrier()
            if "yf" in debug_taps:
                reg_tap("yf", [128, HBC_C * 1027], BF16)
                P.op("sync", lambda e: e.dma_start(out=taps["yf"], in_=bufB[:]),
                     inc="TAP", n=16)
                P.op("sync", "WAITONLY", waits=[("TAP", P.val("TAP"))])
                P.barrier()

        with ExitStack() as ph:
            def sbp(name, shape, dt=F32):
                return ph.enter_context(nc.sbuf_tensor(name, shape, dt))
            yf2 = sbp("yf2", [128, 1024], BF16)
            row1 = sbp("row1", [1, 512], F32)
            row2 = sbp("row2", [1, 512], F32)
            rrep = sbp("rrep", [128, 1024], F32)
            p_var = ph.enter_context(nc.psum_tensor("p_var", [128, 512], F32))
            p_rr = ph.enter_context(nc.psum_tensor("p_rr", [128, 512], F32))

            for half in range(2):
                for fc in range(IC):
                    w = []
                    if fc >= 2:
                        w.append(("RMM", P.val("RMM") - 1))
                    P.op("scalar", lambda e, fc=fc, half=half: e.activation(
                        yf2[:, 512 * (fc % 2):512 * (fc % 2) + 512],
                        bufB[:, 1027 * fc + HALO + 512 * half:
                             1027 * fc + HALO + 512 * half + 512],
                        AF.Square), waits=w, inc="RSQ")
                    sv = P.val("RSQ")
                    w = [("RSQ", sv)]
                    if half == 1 and fc == 0:
                        w.append(("RRS", 2 * half))
                    P.op("tensor", lambda e, fc=fc: e.matmul(
                        p_var[0:1, :], onesb[:],
                        yf2[:, 512 * (fc % 2):512 * (fc % 2) + 512],
                        start=(fc == 0), stop=(fc == IC - 1),
                        skip_group_check=True),
                        waits=w, inc="RMM")
                mv = P.val("RMM")
                P.op("scalar", lambda e: e.activation(
                    row1[:], p_var[0:1, :], AF.Ln, scale=1.0 / INTER,
                    bias=fcst[0:1, 0:1]),
                    waits=[("RMM", mv)], inc="RRS")
                rr_ = P.val("RRS")
                P.op("scalar", lambda e: e.activation(
                    row2[:], row1[:], AF.Exp, scale=-0.5),
                    waits=[("RRS", rr_)], inc="RRS")
                rv = P.val("RRS")
                w = [("RRS", rv)]
                if half == 1:
                    w.append(("RRC", half))
                P.op("tensor", lambda e: e.matmul(
                    p_rr[:, :], ones1[:], row2[0:1, :], start=True, stop=True,
                    skip_group_check=True),
                    waits=w, inc="RMM")
                mv = P.val("RMM")
                P.op("vector", lambda e, half=half: e.tensor_copy(
                    rrep[:, 512 * half:512 * half + 512], p_rr[:, :]),
                    waits=[("RMM", mv)], inc="RRC")
            rv = P.val("RRC")
            for fc in range(IC):
                P.op("vector", lambda e, fc=fc: e.scalar_tensor_tensor(
                    bufA[:, 1024 * fc:1024 * fc + 1024],
                    bufB[:, 1027 * fc + HALO:1027 * fc + HALO + 1024],
                    normw[:, fc:fc + 1], rrep[:], ALU.mult, ALU.mult),
                    waits=[("RRC", rv)], inc="RYN")
            P.barrier()
            if "yn" in debug_taps:
                reg_tap("yn", [128, GATE_C * 1024], BF16)
                P.op("sync", lambda e: e.dma_start(out=taps["yn"], in_=bufA[:]),
                     inc="TAP", n=16)
                P.op("sync", "WAITONLY", waits=[("TAP", P.val("TAP"))])
                P.barrier()

        with ExitStack() as ph:
            owf = [ph.enter_context(nc.sbuf_tensor(f"owf{i}", [128, 2048], F32))
                  for i in range(2)]
            owb = [ph.enter_context(nc.sbuf_tensor(f"owb{i}", [128, 2048], BF16))
                  for i in range(2)]
            opps = [ph.enter_context(nc.psum_tensor(f"opps{i}", [128, 512], F32))
                   for i in range(4)]
            ui = 0
            for hc in range(DHC):
                for kh in range(2):
                    b = ui % 2
                    w = [("OCA", ui - 1)] if ui >= 2 else []
                    P.op("sync", lambda e, hc=hc, kh=kh, b=b: e.dma_start(
                        out=owf[b][:],
                        in_=d_wout[hc, 16 * kh:16 * kh + 16].rearrange(
                            "k p m -> p k m")),
                        waits=w, inc="ODW", n=16)
                    w = [("ODW", 16 * (ui + 1))]
                    if ui >= 2:
                        w.append(("OMQ", 2 * (ui - 1)))
                    P.op("scalar", lambda e, b=b: e.copy(
                        owb[b][:], owf[b][:]), waits=w, inc="OCA")
                    cav = P.val("OCA")
                    for tt in range(2):
                        w = [("OCA", cav)] if tt == 0 else []
                        if kh == 0 and hc >= 2:
                            w.append(("OEV", 2 * (hc - 1)))
                        slot = (hc % 2) * 2 + tt
                        def omm(e, b=b, kh=kh, tt=tt, slot=slot):
                            last = None
                            for k in range(16):
                                kk = 16 * kh + k
                                last = e.matmul(
                                    opps[slot][:, :],
                                    owb[b][:, 128 * k:128 * k + 128],
                                    bufA[:, 1024 * kk + 512 * tt:
                                         1024 * kk + 512 * tt + 512],
                                    start=(kk == 0), stop=(kk == 31),
                                    skip_group_check=True)
                            return last
                        P.op("tensor", omm, waits=w, inc="OMQ")
                        mqv = P.val("OMQ")
                        if kh == 1:
                            P.op("vector", lambda e, hc=hc, tt=tt, slot=slot:
                                 e.tensor_copy(
                                     bufB[:, 1024 * hc + 512 * tt:
                                          1024 * hc + 512 * tt + 512],
                                     opps[slot][:, :]),
                                 waits=[("OMQ", mqv)], inc="OEV")
                    ui += 1
            P.barrier()
            if "s_attn" in debug_taps:
                reg_tap("s_attn", [128, DHC * 1024], BF16)
                P.op("sync", lambda e: e.dma_start(
                    out=taps["s_attn"], in_=bufB[:, 0:DHC * 1024]), inc="TAP", n=16)
                P.barrier()

        with ExitStack() as ph:
            def sbp(name, shape, dt=F32):
                return ph.enter_context(nc.sbuf_tensor(name, shape, dt))
            w8g = [sbp(f"w8g{i}", [128, 2048], I8) for i in range(2)]
            w8u = [sbp(f"w8u{i}", [128, 2048], I8) for i in range(2)]
            wbg = [sbp(f"wbg{i}", [128, 2048], BF16) for i in range(2)]
            wbu = [sbp(f"wbu{i}", [128, 2048], BF16) for i in range(2)]
            gb = sbp("gb", [128, 512], BF16)
            ub = sbp("ub", [128, 512], BF16)
            p_g = ph.enter_context(nc.psum_tensor("p_g", [128, 512], F32))
            p_u = ph.enter_context(nc.psum_tensor("p_u", [128, 512], F32))
            p_d = [ph.enter_context(nc.psum_tensor(f"p_d{i}", [128, 512], F32))
                   for i in range(2)]

            for half in range(2):
                for ic in range(FIC):
                    ui = half * FIC + ic
                    b = ui % 2
                    w = []
                    if ui >= 2:
                        w.append(("FCA", 2 * ui - 2))
                    if half == 1 and ic < 2:
                        w.append(("GCA", 64))
                    P.op("sync", lambda e, ic=ic, b=b: e.dma_start(
                        out=w8g[b][:], in_=d_wg[ic].rearrange("k p m -> p k m")),
                        waits=w, inc="FDW", n=16)
                    P.op("sync", lambda e, ic=ic, b=b: e.dma_start(
                        out=w8u[b][:], in_=d_wu[ic].rearrange("k p m -> p k m")),
                        inc="FDW", n=16)
                    dwv = P.val("FDW")
                    w = [("FDW", dwv)]
                    if ui >= 2:
                        w.append(("FMQ", 2 * (ui - 1)))
                    P.op("scalar", lambda e, b=b: e.copy(wbg[b][:], w8g[b][:]),
                         waits=w, inc="FCA")
                    P.op("scalar", lambda e, b=b: e.copy(wbu[b][:], w8u[b][:]),
                         inc="FCA")
                    cav = P.val("FCA")
                    w = [("FCA", cav)]
                    if ui >= 1:
                        w.append(("FEV", 3 * ui))
                    def gmm2(e, b=b, half=half):
                        last = None
                        for k in range(EC):
                            last = e.matmul(p_g[:, :], wbg[b][:, 128 * k:128 * k + 128],
                                            xn_sl(k, 512 * half, 512),
                                            start=(k == 0), stop=(k == EC - 1))
                        return last
                    P.op("tensor", gmm2, waits=w, inc="FMQ")
                    def umm(e, b=b, half=half):
                        last = None
                        for k in range(EC):
                            last = e.matmul(p_u[:, :], wbu[b][:, 128 * k:128 * k + 128],
                                            xn_sl(k, 512 * half, 512),
                                            start=(k == 0), stop=(k == EC - 1))
                        return last
                    P.op("tensor", umm, inc="FMQ")
                    mqv = P.val("FMQ")
                    w = [("FMQ", mqv - 1)]
                    if ui >= 1:
                        w.append(("FEV", 3 * ui))
                    P.op("scalar", lambda e, ic=ic: e.activation(
                        gb[:], p_g[:, :], AF.Silu, scale=sgate[:, ic:ic + 1]),
                        waits=w, inc="FEV")
                    P.op("vector", lambda e, ic=ic: e.tensor_scalar_mul(
                        ub[:], p_u[:, :], sup[:, ic:ic + 1]),
                        waits=[("FMQ", mqv)], inc="FEV")
                    ev = P.val("FEV")
                    w = [("FEV", ev)]
                    if half == 1 and ic == 0:
                        w.append(("GMQ", 64))
                    P.op("vector", lambda e, ic=ic: e.tensor_tensor(
                        bufA[:, 512 * ic:512 * ic + 512], gb[:], ub[:], ALU.mult),
                        waits=w, inc="FEV")
                fev = P.val("FEV")
                for hc in range(DHC):
                    dui = half * DHC + hc
                    for kq in range(4):
                        uu = 4 * dui + kq
                        b = uu % 2
                        w = [("GCA", uu - 1)] if uu >= 2 else []
                        if kq == 0 and hc == 0:
                            w.append(("FEV", fev))
                        P.op("sync", lambda e, hc=hc, kq=kq, b=b: e.dma_start(
                            out=w8g[b][:],
                            in_=d_wd[hc, 16 * kq:16 * kq + 16].rearrange(
                                "k p m -> p k m")),
                            waits=w, inc="GDW", n=16)
                        w = [("GDW", 16 * (uu + 1))]
                        if uu >= 2:
                            w.append(("GMQ", uu - 1))
                        P.op("scalar", lambda e, b=b: e.copy(
                            wbg[b][:], w8g[b][:]), waits=w, inc="GCA")
                        cav = P.val("GCA")
                        w = [("GCA", cav), ("FEV", fev)]
                        if dui >= 2 and kq == 0:
                            w.append(("GEV", dui - 1))
                        def dmm(e, b=b, kq=kq, dui=dui):
                            last = None
                            for k in range(16):
                                kk = 16 * kq + k
                                last = e.matmul(
                                    p_d[dui % 2][:, :],
                                    wbg[b][:, 128 * k:128 * k + 128],
                                    bufA[:, 512 * kk:512 * kk + 512],
                                    start=(kk == 0), stop=(kk == 63),
                                    skip_group_check=True)
                            return last
                        P.op("tensor", dmm, waits=w, inc="GMQ")
                    mqv = P.val("GMQ")
                    P.op("scalar", lambda e, hc=hc, half=half, dui=dui:
                         e.activation(
                             bufB[:, DHC * 1024 + 1024 * hc + 512 * half:
                                  DHC * 1024 + 1024 * hc + 512 * half + 512],
                             p_d[dui % 2][:, :], AF.Silu,
                             scale=sdown[:, hc:hc + 1]),
                         waits=[("GMQ", mqv)], inc="GEV")
            P.barrier()
            if "g_ffn" in debug_taps:
                reg_tap("g_ffn", [128, DHC * 1024], BF16)
                P.op("sync", lambda e: e.dma_start(
                    out=taps["g_ffn"], in_=bufB[:, DHC * 1024:2 * DHC * 1024]),
                    inc="TAP", n=16)
                P.op("sync", "WAITONLY", waits=[("TAP", P.val("TAP"))])
                P.barrier()

        with ExitStack() as ph:
            def sbp(name, shape, dt=F32):
                return ph.enter_context(nc.sbuf_tensor(name, shape, dt))
            xb = [sbp(f"fxb{i}", [128, 512], F32) for i in range(2)]
            ot = [sbp(f"fot{i}", [128, 512], F32) for i in range(2)]
            sgp = [sbp(f"fsg{i}", [128, 128], BF16) for i in range(2)]
            p_t = [ph.enter_context(nc.psum_tensor(f"fpt{i}", [128, 512], BF16))
                   for i in range(2)]
            un = 0
            for tt in range(8):
                for fq in range(4):
                    b = un % 2
                    w = [("FAD", un - 1)] if un >= 2 else []
                    P.op("sync", lambda e, tt=tt, fq=fq, b=b: e.dma_start(
                        out=xb[b][:], in_=dx[128 * tt:128 * tt + 128,
                                            512 * fq:512 * fq + 512]),
                        waits=w, inc="FXD", n=16)
                    for sub in range(4):
                        fc = 4 * fq + sub
                        w = []
                        if sub >= 1:
                            w.append(("FTR", 4 * un + sub))
                        elif un >= 2:
                            w.append(("FTR", 4 * (un - 1)))
                        P.op("vector", lambda e, fc=fc, tt=tt, b=b: e.tensor_tensor(
                            sgp[b][:], bufB[:, 1024 * fc + 128 * tt:
                                            1024 * fc + 128 * tt + 128],
                            bufB[:, DHC * 1024 + 1024 * fc + 128 * tt:
                                 DHC * 1024 + 1024 * fc + 128 * tt + 128],
                            ALU.mult),
                            waits=w, inc="FSG")
                        sgv = P.val("FSG")
                        w = [("FSG", sgv)]
                        if un >= 2 and sub == 0:
                            w.append(("FAD", un - 1))
                        P.op("tensor", lambda e, sub=sub, b=b: e.transpose(
                            p_t[b][:, 128 * sub:128 * sub + 128], sgp[b][:],
                            idb[:]),
                            waits=w, inc="FTR")
                    trv = P.val("FTR")
                    xdv = P.val("FXD")
                    w = [("FTR", trv), ("FXD", xdv)]
                    if un >= 2:
                        w.append(("FOT", 16 * (un - 1)))
                    P.op("vector", lambda e, b=b: e.tensor_tensor(
                        ot[b][:], p_t[b][:, :], xb[b][:], ALU.add),
                        waits=w, inc="FAD")
                    fav = P.val("FAD")
                    P.op("sync", lambda e, tt=tt, fq=fq, b=b: e.dma_start(
                        out=d_out[128 * tt:128 * tt + 128, 512 * fq:512 * fq + 512],
                        in_=ot[b][:]),
                        waits=[("FAD", fav)], inc="FOT", n=16)
                    un += 1
            P.op("sync", "WAITONLY", waits=[("FOT", P.val("FOT"))])
            if taps:
                P.op("sync", "WAITONLY", waits=[("TAP", P.val("TAP"))])

        with nc.Block() as block:
            P.emit(block)

    return nc, sorted(taps.keys())



def _tile_lhsT(w):
    OUT, IN = w.shape
    OC, KC = OUT // 128, IN // 128
    return np.ascontiguousarray(w.reshape(OC, 128, KC, 128).transpose(0, 2, 3, 1))


def _tile_rhsT(w):
    IN, OUT = w.shape
    KC, OC = IN // 128, OUT // 128
    return np.ascontiguousarray(w.reshape(KC, 128, OC, 128).transpose(2, 0, 1, 3))


def _host_prep(inputs):
    W_in = np.asarray(inputs["L0_attn_W_in"], np.float32)
    W_out = np.asarray(inputs["L0_attn_W_out"], np.float32)
    wg = np.asarray(inputs["L0_ffn_w_gate"], np.int8)
    wu = np.asarray(inputs["L0_ffn_w_up"], np.int8)
    wd = np.asarray(inputs["L0_ffn_w_down"], np.int8)
    convw = np.asarray(inputs["L0_attn_conv_w"], np.float32)
    convb = np.asarray(inputs["L0_attn_conv_b"], np.float32)
    normw = np.asarray(inputs["L0_attn_norm_w"], np.float32)
    s_g = np.asarray(inputs["L0_ffn_s_gate"], np.float32)
    s_u = np.asarray(inputs["L0_ffn_s_up"], np.float32)
    s_d = np.asarray(inputs["L0_ffn_s_down"], np.float32)
    dtbv = np.asarray(inputs["L0_attn_dt_bias"], np.float32)
    alogv = np.asarray(inputs["L0_attn_A_log"], np.float32)
    D = np.asarray(inputs["L0_attn_D"], np.float32)

    W_in_pad = np.zeros((PC * 128, E), np.float32)
    W_in_pad[:PROJ] = W_in
    d = {}
    d["w_in_img"] = _tile_lhsT(W_in_pad)
    d["w_out_img"] = _tile_lhsT(W_out)
    d["wg_img"] = _tile_rhsT(wg)
    d["wu_img"] = _tile_rhsT(wu)
    d["wd_img"] = _tile_rhsT(wd)
    d["convw"] = np.ascontiguousarray(
        convw.reshape(HBC_C, 128, 4).transpose(1, 0, 2).reshape(128, HBC_C * 4))
    d["convb"] = np.ascontiguousarray(convb.reshape(HBC_C, 128).T)
    d["normw"] = np.ascontiguousarray(normw.reshape(IC, 128).T)
    d["sgate"] = np.ascontiguousarray(s_g.reshape(FIC, 128).T)
    d["sup"] = np.ascontiguousarray(s_u.reshape(FIC, 128).T)
    d["sdown"] = np.ascontiguousarray(s_d.reshape(DHC, 128).T)
    d["dtb"] = np.ascontiguousarray(dtbv.reshape(NH, 1))
    d["alog"] = np.ascontiguousarray(alogv.reshape(NH, 1))
    p = np.arange(128)[:, None]
    jj = np.arange(IC)[None, :]
    d["dpair"] = np.ascontiguousarray(D[2 * jj + p // 64].astype(np.float32))
    d["idf"] = np.eye(128, dtype=np.float32)
    d["idb"] = np.eye(128, dtype=np.float32).astype(BF)
    s_i = np.arange(128)[:, None]
    l_i = np.arange(256)[None, :]
    m0 = (s_i <= l_i).astype(np.float32)
    m1 = (s_i + 128 <= l_i).astype(np.float32)
    d["gmask"] = np.ascontiguousarray(np.concatenate([m0, m1], 1).astype(BF))
    fc = np.zeros((128, 2), np.float32); fc[:, 0] = EPS; fc[:, 1] = 1.0
    d["fconst"] = fc
    return d


def _in_maps(inputs, shared=None):
    X = np.asarray(inputs["L0_attn_input_states"], np.float32)
    if shared is None:
        shared = _host_prep(inputs)
    maps = []
    for c in range(8):
        b, h = c // 2, c % 2
        m = dict(shared)
        m["x"] = np.ascontiguousarray(X[b, 1024 * h:1024 * h + 1024])
        if h == 0:
            m["xh"] = np.zeros((HALO, E), np.float32)
        else:
            m["xh"] = np.ascontiguousarray(X[b, 1021:1024])
        maps.append(m)
    return maps


def kernel(**inputs):
    _ = inputs.get("L0_attn_mask")
    nc, _unused = build_nc()
    in_maps = _in_maps(inputs)
    res = run_bass_kernel_spmd(nc, in_maps, core_ids=list(range(8)))
    out = np.empty((4, 2048, E), np.float32)
    for c in range(8):
        b, h = c // 2, c % 2
        out[b, 1024 * h:1024 * h + 1024] = res.results[c]["out"]
    return out


# version 12
# speedup vs baseline: 1.0382x; 1.0382x over previous
import numpy as np
import ml_dtypes
from contextlib import ExitStack

import concourse.bass as bass
import concourse.mybir as mybir
from concourse.bass_utils import run_bass_kernel_spmd

F32 = mybir.dt.float32
BF16 = mybir.dt.bfloat16
I8 = mybir.dt.int8
AF = mybir.ActivationFunctionType
ALU = mybir.AluOpType
AXX = mybir.AxisListType.X

BF = ml_dtypes.bfloat16

E = 2048
T = 1024
HALO = 3
NH = 64
CS = 256
NCH = T // CS
PROJ = 8512
PC = 67
GATE_C = 32
HBC_C = 34
INTER = 4096
IC = 32
FI = 8192
FIC = 64
DHC = 16
EC = 16
EPS = 1e-5


class Op:
    __slots__ = ("eng", "fn", "waits", "incs")
    def __init__(self, eng, fn, waits, incs):
        self.eng, self.fn, self.waits, self.incs = eng, fn, waits, incs


class Prog:
    ENGINES = ("sync", "tensor", "vector", "scalar", "gpsimd")

    def __init__(self, nc, es):
        self.nc = nc
        self.ops = []
        self.sems = {}
        self.cnt = {}
        self.es = es

    def sem(self, name):
        if name not in self.sems:
            self.sems[name] = self.es.enter_context(self.nc.semaphore(name))
            self.cnt[name] = 0
        return self.sems[name]

    def val(self, name):
        self.sem(name)
        return self.cnt.get(name, 0)

    def op(self, eng, fn, waits=(), inc=None, n=1):
        incs = []
        if inc is not None:
            self.sem(inc)
            self.cnt[inc] += n
            incs = [(inc, n)]
        ws = []
        for s, v in waits:
            self.sem(s)
            if v > 0:
                ws.append((s, v))
        self.ops.append(Op(eng, fn, ws, incs))

    def barrier(self):
        self.sem("PH")
        for e in self.ENGINES:
            self.cnt["PH"] += 1
            fn = None if e == "sync" else "DRAIN"
            self.ops.append(Op(e, fn, [], [("PH", 1)]))
        v = self.cnt["PH"]
        for e in self.ENGINES:
            self.ops.append(Op(e, "WAITONLY", [("PH", v)], []))

    def replay(self, engname, e):
        for o in self.ops:
            if o.eng != engname:
                continue
            for s, v in o.waits:
                e.wait_ge(self.sems[s], v)
            if o.fn == "WAITONLY":
                continue
            if o.fn == "DRAIN":
                inst = e.drain()
            else:
                inst = e.nop() if o.fn is None else o.fn(e)
            for s, n in o.incs:
                if inst is None:
                    inst = e.nop()
                inst.then_inc(self.sems[s], n)

    def emit(self, block):
        block.sync(lambda e: self.replay("sync", e))
        block.tensor(lambda e: self.replay("tensor", e))
        block.vector(lambda e: self.replay("vector", e))
        block.scalar(lambda e: self.replay("scalar", e))
        block.gpsimd(lambda e: self.replay("gpsimd", e))


def build_nc(debug_taps=()):
    nc = bass.Bass("TRN2", target_bir_lowering=False, debug=False)

    dx = nc.dram_tensor("x", [T, E], F32, kind="ExternalInput").ap()
    dxh = nc.dram_tensor("xh", [HALO, E], F32, kind="ExternalInput").ap()
    d_win = nc.dram_tensor("w_in_img", [PC, 16, 128, 128], F32, kind="ExternalInput").ap()
    d_wout = nc.dram_tensor("w_out_img", [DHC, 32, 128, 128], F32, kind="ExternalInput").ap()
    d_wg = nc.dram_tensor("wg_img", [FIC, 16, 128, 128], I8, kind="ExternalInput").ap()
    d_wu = nc.dram_tensor("wu_img", [FIC, 16, 128, 128], I8, kind="ExternalInput").ap()
    d_wd = nc.dram_tensor("wd_img", [DHC, 64, 128, 128], I8, kind="ExternalInput").ap()
    d_convw = nc.dram_tensor("convw", [128, HBC_C * 4], F32, kind="ExternalInput").ap()
    d_convb = nc.dram_tensor("convb", [128, HBC_C], F32, kind="ExternalInput").ap()
    d_normw = nc.dram_tensor("normw", [128, IC], F32, kind="ExternalInput").ap()
    d_sg = nc.dram_tensor("sgate", [128, FIC], F32, kind="ExternalInput").ap()
    d_su = nc.dram_tensor("sup", [128, FIC], F32, kind="ExternalInput").ap()
    d_sd = nc.dram_tensor("sdown", [128, DHC], F32, kind="ExternalInput").ap()
    d_dtb = nc.dram_tensor("dtb", [NH, 1], F32, kind="ExternalInput").ap()
    d_alog = nc.dram_tensor("alog", [NH, 1], F32, kind="ExternalInput").ap()
    d_dpair = nc.dram_tensor("dpair", [128, IC], F32, kind="ExternalInput").ap()
    d_idf = nc.dram_tensor("idf", [128, 128], F32, kind="ExternalInput").ap()
    d_idb = nc.dram_tensor("idb", [128, 128], BF16, kind="ExternalInput").ap()
    d_gmask = nc.dram_tensor("gmask", [128, 512], BF16, kind="ExternalInput").ap()
    d_fc = nc.dram_tensor("fconst", [128, 2], F32, kind="ExternalInput").ap()
    d_out = nc.dram_tensor("out", [T, E], F32, kind="ExternalOutput").ap()

    taps = {}
    def reg_tap(name, shape, dt):
        taps[name] = nc.dram_tensor("tap_" + name, shape, dt, kind="ExternalOutput").ap()

    with ExitStack() as es:
        P = Prog(nc, es)

        def sb(name, shape, dt=F32):
            return es.enter_context(nc.sbuf_tensor(name, shape, dt))

        xn = sb("xn", [128, EC * 1024], BF16)
        xnh = sb("xnh", [128, EC * HALO], BF16)
        bufA = sb("bufA", [128, GATE_C * 1024], BF16)
        bufB = sb("bufB", [128, HBC_C * 1027], BF16)
        dt_raw = sb("dt_raw", [NH, T], F32)
        convw = sb("convw_s", [128, HBC_C * 4], F32)
        convb = sb("convb_s", [128, HBC_C], F32)
        normw = sb("normw_s", [128, IC], F32)
        sgate = sb("sgate_s", [128, FIC], F32)
        sup = sb("sup_s", [128, FIC], F32)
        sdown = sb("sdown_s", [128, DHC], F32)
        dtb = sb("dtb_s", [NH, 1], F32)
        alog = sb("alog_s", [NH, 1], F32)
        dpair = sb("dpair_s", [128, IC], F32)
        idf = sb("idf_s", [128, 128], F32)
        idb = sb("idb_s", [128, 128], BF16)
        gmask = sb("gmask_s", [128, 512], BF16)
        negA = sb("negA", [NH, 1], F32)
        expa = sb("expa", [NH, 1], F32)
        ones1 = sb("ones1", [1, 128], F32)
        onesb = sb("onesb", [128, 1], BF16)
        zer_c = sb("zer_c", [NH, CS], F32)
        fcst = sb("fcst", [128, 2], F32)

        def xn_sl(k, c0, w):
            return xn[:, 1024 * k + c0: 1024 * k + c0 + w]
        def xnh_sl(k):
            return xnh[:, HALO * k: HALO * k + HALO]
        def hbc_sl(cc, c0, w):
            return bufB[:, 1027 * cc + c0: 1027 * cc + c0 + w]
        def gate_sl(j, c0, w):
            return bufA[:, 1024 * j + c0: 1024 * j + c0 + w]

        for dst, src in [(convw, d_convw), (convb, d_convb), (normw, d_normw),
                         (sgate, d_sg), (sup, d_su), (sdown, d_sd),
                         (dtb, d_dtb), (alog, d_alog), (dpair, d_dpair),
                         (idf, d_idf), (idb, d_idb), (gmask, d_gmask),
                         (fcst, d_fc)]:
            P.op("sync", lambda e, d=dst, s=src: e.dma_start(out=d[:], in_=s),
                 inc="DC", n=16)
        dcv = P.val("DC")
        P.op("vector", lambda e: e.memset(ones1[:], 1.0))
        P.op("vector", lambda e: e.memset(onesb[:], 1.0))
        P.op("vector", lambda e: e.memset(zer_c[:], 0.0))
        P.op("scalar", lambda e: e.activation(expa[:], alog[:], AF.Exp),
             waits=[("DC", dcv)], inc="INI")
        P.op("vector", lambda e: e.tensor_scalar_mul(negA[:], expa[:], -1.0),
             waits=[("INI", 1)], inc="INI")

        with ExitStack() as ph:
            def sbp(name, shape, dt=F32):
                return ph.enter_context(nc.sbuf_tensor(name, shape, dt))
            xt = [sbp(f"xt{i}", [128, E], F32) for i in range(2)]
            xht = sbp("xht", [128, E], F32)
            sqt = sbp("sqt", [128, E], BF16)
            s1 = sbp("s1", [128, 1], F32)
            ssq = sbp("ssq", [128, 1], F32)
            mcol = sbp("mcol", [128, 1], F32)
            mm2 = sbp("mm2", [128, 1], F32)
            vcol = sbp("vcol", [128, 1], F32)
            rstd = sbp("rstd", [128, 1], F32)
            mb = sbp("mb", [128, 1], F32)
            xn_tm = sbp("xn_tm", [128, E], BF16)
            stt_dump = sbp("stt_dump", [128, 7], F32)
            p_tr = ph.enter_context(nc.psum_tensor("p_tr", [128, 1024], BF16))

            def ln_tile(i, is_halo):
                buf = xht if is_halo else xt[i % 2]
                src = dxh if is_halo else dx[128 * i:128 * i + 128, :]
                npart = 128
                bb = buf[:]
                w = [("LNT", 2 * (i - 1))] if (not is_halo and i >= 2) else []
                if is_halo:
                    P.op("vector", lambda e: e.memset(xht[:], 0.0), inc="LNH")
                    P.op("sync", lambda e: e.dma_start(out=xht[0:HALO, :], in_=src),
                         waits=[("LNH", 1)], inc="DX", n=16)
                else:
                    P.op("sync", lambda e: e.dma_start(out=bb, in_=src),
                         waits=w, inc="DX", n=16)
                ldv = P.val("DX")
                w = [("DX", ldv)]
                if not is_halo and i == 0:
                    w.append(("DC", P.val("DC")))
                if not is_halo and i >= 1:
                    w.append(("LNV", 5 * (i - 1) + 4))
                elif is_halo:
                    w.append(("LNV", 5 * 7 + 4))
                P.op("scalar", lambda e: e.activation(
                    sqt[0:npart, :], bb, AF.Square, accum_out=ssq[0:npart, :]),
                    waits=w, inc="LNS")
                P.op("vector", lambda e: e.reduce_sum(s1[0:npart, :], bb, axis=AXX),
                     waits=[("DX", ldv)], inc="LNV")
                sv = P.val("LNS")
                rdv = P.val("LNV")
                P.op("vector", lambda e: e.tensor_scalar_mul(
                    mcol[0:npart, :], s1[0:npart, :], 1.0 / E),
                    waits=[("LNV", rdv)], inc="LNV")
                rdv = P.val("LNV")
                P.op("vector", lambda e: e.tensor_mul(
                    mm2[0:npart, :], mcol[0:npart, :], mcol[0:npart, :]),
                    waits=[("LNV", rdv)], inc="LNV")
                rdv = P.val("LNV")
                P.op("vector", lambda e: e.scalar_tensor_tensor(
                    vcol[0:npart, :], ssq[0:npart, :], 1.0 / E, mm2[0:npart, :],
                    ALU.mult, ALU.subtract),
                    waits=[("LNS", sv), ("LNV", rdv)], inc="LNV")
                vv = P.val("LNV")
                P.op("scalar", lambda e: e.activation(
                    rstd[0:npart, :], vcol[0:npart, :], AF.Ln,
                    bias=fcst[0:npart, 0:1]),
                    waits=[("LNV", vv)], inc="LNS")
                lnv_ = P.val("LNS")
                P.op("scalar", lambda e: e.activation(
                    rstd[0:npart, :], rstd[0:npart, :], AF.Exp, scale=-0.5),
                    waits=[("LNS", lnv_)], inc="LNS")
                sv = P.val("LNS")
                P.op("vector", lambda e: e.scalar_tensor_tensor(
                    mb[0:npart, :], mcol[0:npart, :], -1.0, rstd[0:npart, :],
                    ALU.mult, ALU.mult),
                    waits=[("LNS", sv)], inc="LNV")
                vv = P.val("LNV")
                w = [("LNV", vv)]
                pv_prev = P.val("LNP")
                if pv_prev > 0:
                    w.append(("LNP", pv_prev))
                P.op("scalar", lambda e: e.activation(
                    xn_tm[0:npart, :], bb, AF.Identity, bias=mb[0:npart, :],
                    scale=rstd[0:npart, :]),
                    waits=w, inc="LNS")
                sv = P.val("LNS")
                if is_halo:
                    for half in range(2):
                        for k8 in range(8):
                            k = 8 * half + k8
                            w = []
                            if k == 0:
                                w.append(("LNS", sv))
                            if k8 == 0:
                                w.append(("LNT", P.val("LNT")))
                            P.op("tensor", lambda e, k=k, k8=k8: e.transpose(
                                p_tr[:, 128 * k8:128 * k8 + 128],
                                xn_tm[:, 128 * k:128 * k + 128], idb[:]),
                                waits=w, inc="LNP")
                        pv = P.val("LNP")
                        P.op("vector", lambda e, half=half: e.tensor_copy(
                            xnh[:, :].rearrange("p (k c) -> p k c", c=HALO)
                            [:, 8 * half:8 * half + 8, :],
                            p_tr[:, 0:1024].rearrange("p (k c) -> p k c", c=128)
                            [:, :, 0:HALO]),
                            waits=[("LNP", pv)], inc="LNT")
                else:
                    for half in range(2):
                        for k8 in range(8):
                            k = 8 * half + k8
                            w = []
                            if k == 0:
                                w.append(("LNS", sv))
                            if k8 == 0:
                                w.append(("LNT", P.val("LNT")))
                            P.op("tensor", lambda e, k=k, k8=k8: e.transpose(
                                p_tr[:, 128 * k8:128 * k8 + 128],
                                xn_tm[:, 128 * k:128 * k + 128], idb[:]),
                                waits=w, inc="LNP")
                        pv = P.val("LNP")
                        P.op("vector", lambda e, i=i, half=half: e.tensor_copy(
                            xn[:, :].rearrange("p (k t) -> p k t", t=1024)
                              [:, 8 * half:8 * half + 8, 128 * i:128 * i + 128],
                            p_tr[:, 0:1024].rearrange("p (k t) -> p k t", t=128)),
                            waits=[("LNP", pv)], inc="LNT")

            for i in range(8):
                ln_tile(i, False)
            ln_tile(0, True)
            P.barrier()
            if "lnh" in debug_taps:
                reg_tap("lnh_xt", [HALO, E], F32)
                reg_tap("lnh_xntm", [HALO, E], BF16)
                reg_tap("lnh_st", [128, 7], F32)
                for ci, st in enumerate([s1, ssq, mcol, mm2, vcol, rstd, mb]):
                    P.op("vector", lambda e, ci=ci, st=st: e.tensor_copy(
                        stt_dump[:, ci:ci + 1], st[:]), inc="TAP0")
                P.op("sync", lambda e: e.dma_start(out=taps["lnh_xt"], in_=xht[0:HALO, :]),
                     waits=[("TAP0", 7)], inc="TAP", n=16)
                P.op("sync", lambda e: e.dma_start(out=taps["lnh_xntm"],
                     in_=xn_tm[0:HALO, :]), inc="TAP", n=16)
                P.op("sync", lambda e: e.dma_start(out=taps["lnh_st"],
                     in_=stt_dump[:]), inc="TAP", n=16)
                P.barrier()
            if "xn" in debug_taps:
                reg_tap("xn", [128, EC * 1024], BF16)
                reg_tap("xnh", [128, EC * HALO], BF16)
                P.op("sync", lambda e: e.dma_start(out=taps["xn"], in_=xn[:]),
                     inc="TAP", n=16)
                P.op("sync", lambda e: e.dma_start(out=taps["xnh"], in_=xnh[:]),
                     inc="TAP", n=16)
                P.op("sync", "WAITONLY", waits=[("TAP", P.val("TAP"))])
                P.barrier()

        with ExitStack() as ph:
            wf = [ph.enter_context(nc.sbuf_tensor(f"wf{i}", [128, 2048], F32))
                  for i in range(2)]
            wb = [ph.enter_context(nc.sbuf_tensor(f"wb{i}", [128, 2048], BF16))
                  for i in range(2)]
            pps = [ph.enter_context(nc.psum_tensor(f"pps{i}", [128, 512], F32))
                   for i in range(4)]
            acc = [ph.enter_context(nc.sbuf_tensor(f"acc{i}", [128, T], F32))
                   for i in range(2)]

            def emit_conv(cc):
                a0, a1 = acc[0], acc[1]
                w = [("PEV", 3 * (cc + 1))]
                if cc >= 1:
                    w.append(("CVS", cc))
                P.op("vector", lambda e, cc=cc, a0=a0: e.tensor_scalar(
                    a0[:], hbc_sl(cc, 0, T), convw[:, 4 * cc:4 * cc + 1],
                    convb[:, cc:cc + 1], ALU.mult, ALU.add),
                    waits=w, inc="CV0")
                v0 = P.val("CV0")
                P.op("vector", lambda e, cc=cc, a0=a0, a1=a1: e.scalar_tensor_tensor(
                    a1[:], hbc_sl(cc, 1, T), convw[:, 4 * cc + 1:4 * cc + 2], a0[:],
                    ALU.mult, ALU.add),
                    waits=[("CV0", v0)], inc="CV1")
                v1 = P.val("CV1")
                P.op("vector", lambda e, cc=cc, a0=a0, a1=a1: e.scalar_tensor_tensor(
                    a0[:], hbc_sl(cc, 2, T), convw[:, 4 * cc + 2:4 * cc + 3], a1[:],
                    ALU.mult, ALU.add),
                    waits=[("CV1", v1)], inc="CV2")
                v2 = P.val("CV2")
                P.op("vector", lambda e, cc=cc, a0=a0, a1=a1: e.scalar_tensor_tensor(
                    a1[:], hbc_sl(cc, 3, T), convw[:, 4 * cc + 3:4 * cc + 4], a0[:],
                    ALU.mult, ALU.add),
                    waits=[("CV2", v2)], inc="CV3")
                v3 = P.val("CV3")
                P.op("scalar", lambda e, cc=cc, a1=a1: e.activation(
                    hbc_sl(cc, HALO, T), a1[:], AF.Silu),
                    waits=[("CV3", v3)], inc="CVS")

            order = list(range(32, 67)) + list(range(32))
            cumg = [0]
            for fc_ in order:
                cumg.append(cumg[-1] + (3 if 32 <= fc_ <= 65 else 2))
            grp = 0
            for oi, fc in enumerate(order):
                w = [("PCA", oi - 1)] if oi >= 2 else []
                P.op("sync", lambda e, fc=fc, b=oi % 2: e.dma_start(
                    out=wf[b][:], in_=d_win[fc].rearrange("k p m -> p k m")),
                    waits=w, inc="PDW", n=16)
                w = [("PDW", 16 * (oi + 1))]
                if oi >= 2:
                    w.append(("PMQ", cumg[oi - 1]))
                P.op("scalar", lambda e, b=oi % 2: e.copy(wb[b][:], wf[b][:]),
                     waits=w, inc="PCA")
                cav = P.val("PCA")
                groups = [(HALO, 512), (HALO + 512, 512)]
                if 32 <= fc <= 65:
                    groups.append((0, HALO))
                for gi, (c0, wdt) in enumerate(groups):
                    slot = grp % 4
                    wts = [("PCA", cav)] if gi == 0 else []
                    if grp >= 4:
                        wts.append(("PEV", grp - 3))
                    def mmg(e, b=oi % 2, c0=c0, wdt=wdt, slot=slot):
                        last = None
                        for k in range(EC):
                            rhs = xnh_sl(k) if c0 == 0 else xn_sl(k, c0 - HALO, wdt)
                            last = e.matmul(pps[slot][:, 0:wdt],
                                            wb[b][:, 128 * k:128 * k + 128],
                                            rhs, start=(k == 0), stop=(k == EC - 1))
                        return last
                    P.op("tensor", mmg, waits=wts, inc="PMQ")
                    grp += 1
                    mqv = P.val("PMQ")
                    if fc < GATE_C:
                        P.op("scalar", lambda e, fc=fc, c0=c0, wdt=wdt, slot=slot:
                             e.activation(gate_sl(fc, c0 - HALO, wdt),
                                          pps[slot][:, 0:wdt], AF.Silu),
                             waits=[("PMQ", mqv)], inc="PEV")
                    elif fc < 66:
                        P.op("vector", lambda e, cc=fc - 32, c0=c0, wdt=wdt, slot=slot:
                             e.tensor_copy(hbc_sl(cc, c0, wdt), pps[slot][:, 0:wdt]),
                             waits=[("PMQ", mqv)], inc="PEV")
                    else:
                        P.op("vector", lambda e, c0=c0, wdt=wdt, slot=slot:
                             e.tensor_copy(dt_raw[:, c0 - HALO:c0 - HALO + wdt],
                                           pps[slot][0:NH, 0:wdt]),
                             waits=[("PMQ", mqv)], inc="PEV")
                if oi == 35:
                    for cc_ in range(HBC_C):
                        emit_conv(cc_)
            P.barrier()
            if "hbc_raw" in debug_taps:
                reg_tap("hbc_raw", [128, HBC_C * 1027], BF16)
                P.op("sync", lambda e: e.dma_start(out=taps["hbc_raw"], in_=bufB[:]),
                     inc="TAP", n=16)
            if "gate" in debug_taps:
                reg_tap("gate", [128, GATE_C * 1024], BF16)
                P.op("sync", lambda e: e.dma_start(out=taps["gate"], in_=bufA[:]),
                     inc="TAP", n=16)
            if "dt_raw" in debug_taps:
                reg_tap("dt_raw", [NH, T], F32)
                P.op("sync", lambda e: e.dma_start(out=taps["dt_raw"], in_=dt_raw[:]),
                     inc="TAP", n=16)
            if debug_taps:
                P.op("sync", "WAITONLY", waits=[("TAP", P.val("TAP"))])
                P.barrier()

        with ExitStack() as ph:
            def sbp(name, shape, dt=F32):
                return ph.enter_context(nc.sbuf_tensor(name, shape, dt))
            htm = [sbp(f"htm{s}", [128, INTER], BF16) for s in range(2)]
            acumT = sbp("acumT", [128, 2 * 64], F32)
            dtT = sbp("dtT", [128, 2 * 64], F32)
            ddT = sbp("ddT", [128, 2 * 64], F32)
            btm = sbp("btm", [128, 2 * 128], BF16)
            dt_c = sbp("dt_c", [NH, CS], F32)
            ac_c = sbp("ac_c", [NH, CS], F32)
            sp3 = sbp("sp3", [NH, CS], F32)
            Esb = [sbp(f"Esb{i}", [128, 512], BF16) for i in range(2)]
            MTsb = [sbp(f"MTsb{i}", [128, 512], BF16) for i in range(2)]
            Mtmp = sbp("Mtmp", [128, 512], BF16)
            Gsb = sbp("Gsb", [128, 512], BF16)
            sdr = sbp("sdr", [128, CS], BF16)
            st_sb = sbp("st_sb", [128, 128], BF16)
            hdd = sbp("hdd", [128, 4 * 64], BF16)
            u_a = sbp("u_a", [128, CS], F32)
            u_b = sbp("u_b", [128, CS], F32)
            sp1 = u_a[0:NH, :]
            sp2 = u_b[0:NH, :]
            dec_c = sp1
            dd_c = sp2
            rows = [sbp(f"rows{i}", [1, CS], F32) for i in range(3)]
            p_rep = [ph.enter_context(nc.psum_tensor(f"p_rep{i}", [128, 512], F32))
                     for i in range(2)]
            p_yd = [ph.enter_context(nc.psum_tensor(f"p_yd{i}", [128, 512], F32))
                    for i in range(2)]
            p_yo = ph.enter_context(nc.psum_tensor("p_yo", [128, 512], F32))
            p_st = ph.enter_context(nc.psum_tensor("p_st", [128, 512], F32))
            p_tb = ph.enter_context(nc.psum_tensor("p_tb", [128, 1024], BF16))
            p_tf = ph.enter_context(nc.psum_tensor("p_tf", [128, 512], F32))

            for c in range(NCH):
                c0 = HALO + CS * c
                P.op("vector", lambda e, c=c: e.tensor_scalar_add(
                    sp1, dt_raw[:, CS * c:CS * c + CS], dtb[:]), inc="CP")
                v = P.val("CP")
                P.op("vector", lambda e: e.scalar_tensor_tensor(
                    sp2, sp1, -1.0, sp1, ALU.mult, ALU.max),
                    waits=[("CP", v)], inc="CP")
                v = P.val("CP")
                P.op("scalar", lambda e: e.activation(sp3[:], sp2, AF.Exp,
                                                      scale=-1.0),
                     waits=[("CP", v)], inc="CP")
                v = P.val("CP")
                P.op("scalar", lambda e: e.activation(sp2, sp3[:], AF.Ln,
                                                      bias=fcst[0:NH, 1:2]),
                     waits=[("CP", v)], inc="CP")
                v = P.val("CP")
                P.op("vector", lambda e: e.tensor_scalar_max(sp3[:], sp1, 0.0),
                     waits=[("CP", v)], inc="CP")
                v = P.val("CP")
                P.op("vector", lambda e: e.tensor_add(dt_c[:], sp3[:], sp2),
                     waits=[("CP", v)], inc="CP")
                v = P.val("CP")
                P.op("vector", lambda e: e.tensor_scalar_mul(sp1, dt_c[:],
                                                             negA[:]),
                     waits=[("CP", v)], inc="CP")
                v = P.val("CP")
                P.op("vector", lambda e: e.tensor_tensor_scan(
                    ac_c[:], sp1, zer_c[:], 0.0, ALU.add, ALU.add),
                    waits=[("CP", v)], inc="CP")
                v = P.val("CP")
                P.op("scalar", lambda e: e.activation(
                    dec_c, ac_c[:], AF.Exp, bias=ac_c[:, CS - 1:CS],
                    scale=-1.0), waits=[("CP", v)], inc="CP")
                v = P.val("CP")
                P.op("vector", lambda e: e.tensor_mul(dd_c, dt_c[:], dec_c),
                     waits=[("CP", v)], inc="CP")
                v = P.val("CP")
                for s in range(2):
                    P.op("tensor", lambda e, s=s: e.transpose(
                        p_tf[:, 64 * s:64 * s + 64],
                        ac_c[:, 128 * s:128 * s + 128], idf[0:NH, 0:NH]),
                        waits=[("CP", v)] if s == 0 else [], inc="CPT")
                    P.op("tensor", lambda e, s=s: e.transpose(
                        p_tf[:, 128 + 64 * s:128 + 64 * s + 64],
                        dt_c[:, 128 * s:128 * s + 128], idf[0:NH, 0:NH]),
                        inc="CPT")
                    P.op("tensor", lambda e, s=s: e.transpose(
                        p_tf[:, 256 + 64 * s:256 + 64 * s + 64],
                        dd_c[:, 128 * s:128 * s + 128], idf[0:NH, 0:NH]),
                        inc="CPT")
                tv = P.val("CPT")
                P.op("vector", lambda e: e.tensor_scalar_mul(
                    acumT[:], p_tf[:, 0:128], -1.0),
                     waits=[("CPT", tv)], inc="CP")
                P.op("vector", lambda e: e.tensor_copy(dtT[:], p_tf[:, 128:256]),
                     inc="CP")
                P.op("vector", lambda e: e.tensor_copy(ddT[:], p_tf[:, 256:384]),
                     inc="CP")
                cpv = P.val("CP")
                for s in range(2):
                    P.op("tensor", lambda e, c0=c0, s=s: e.transpose(
                        p_tb[:, 128 * s:128 * s + 128],
                        bufB[:, 1027 * 32 + c0 + 128 * s:
                             1027 * 32 + c0 + 128 * s + 128], idb[:]),
                        waits=[("CP", cpv)] if s == 0 else [], inc="CPT")
                tv = P.val("CPT")
                P.op("vector", lambda e: e.tensor_copy(btm[:], p_tb[:, 0:256]),
                     waits=[("CPT", tv)], inc="CP")
                gv = P.val("CP")
                def gmm(e, c0=c0):
                    last = None
                    for s in range(2):
                        last = e.matmul(
                            p_tf[:, 256 * s:256 * s + 256],
                            bufB[:, 1027 * 32 + c0 + 128 * s:
                                 1027 * 32 + c0 + 128 * s + 128],
                            bufB[:, 1027 * 33 + c0:1027 * 33 + c0 + CS],
                            start=True, stop=True, skip_group_check=True)
                    return last
                P.op("tensor", gmm, waits=[("CP", gv)], inc="CPT")
                tv = P.val("CPT")
                P.op("vector", lambda e: e.tensor_tensor(
                    Gsb[:], p_tf[:, 0:512], gmask[:], ALU.mult),
                    waits=[("CPT", tv)], inc="CP")
                for s in range(2):
                    for fc8 in range(4):
                        for k8 in range(8):
                            fc = 8 * fc8 + k8
                            w = []
                            if s == 0 and fc8 == 0 and k8 == 0:
                                w.append(("CP", P.val("CP")))
                            if k8 == 0:
                                w.append(("CPH", P.val("CPH")))
                            P.op("tensor", lambda e, c0=c0, s=s, fc=fc, k8=k8:
                                 e.transpose(
                                     p_tb[:, 128 * k8:128 * k8 + 128],
                                     bufB[:, 1027 * fc + c0 + 128 * s:
                                          1027 * fc + c0 + 128 * s + 128],
                                     idb[:]),
                                 waits=w, inc="CPT")
                        tv = P.val("CPT")
                        P.op("vector", lambda e, s=s, fc8=fc8: e.tensor_copy(
                            htm[s][:, 1024 * fc8:1024 * fc8 + 1024],
                            p_tb[:, 0:1024]),
                            waits=[("CPT", tv)], inc="CPH")
                cpv = P.val("CP")
                cphv = P.val("CPH")

                q0 = c * IC * 2

                def emit_rows(q, j2, h2):
                    hh = 2 * j2 + h2
                    w = [("CP", cpv)]
                    if q >= 3:
                        w.append(("RL", q - 2))
                    P.op("sync", lambda e, hh=hh, b=q % 3: e.dma_start(
                        out=rows[b][0:1, :], in_=ac_c[hh:hh + 1, :]),
                        waits=w, inc="R16", n=16)

                def emit_repl(q):
                    w = [("R16", 16 * (q - q0 + 1) + r16base)]
                    if q >= 2:
                        w.append(("EXD", 2 * (q - 1)))
                        w.append(("SDR", q - 1))
                    P.op("tensor", lambda e, b=q % 2, rb=q % 3: e.matmul(
                        p_rep[b][:, 0:256], ones1[:], rows[rb][0:1, :],
                        start=True, stop=True, skip_group_check=True),
                        waits=w, inc="RL")

                r16base = P.val("R16")
                rlbase = P.val("RL")
                emit_rows(q0, 0, 0)
                emit_rows(q0 + 1, 0, 1)
                emit_repl(q0)
                for j in range(IC):
                    pi = c * IC + j
                    for h in range(2):
                        q = 2 * pi + h
                        hh = 2 * j + h
                        par = q % 2
                        if q + 1 < q0 + 2 * IC:
                            if q + 2 < q0 + 2 * IC:
                                j2, h2 = divmod(q + 2 - q0, 2)
                                emit_rows(q + 2, (q + 2 - q0) // 2,
                                          (q + 2 - q0) % 2)
                            emit_repl(q + 1)
                        rl = rlbase + (q - q0) + 1
                        for s in range(2):
                            w = [("RL", rl)] if s == 0 else []
                            if q >= 2 and s == 0:
                                w.append(("MDT", 2 * (q - 1)))
                            P.op("scalar", lambda e, s=s, hh=hh, b=par:
                                 e.activation(
                                     Esb[b][:, 256 * s:256 * s + 256],
                                     p_rep[b][:, 0:256], AF.Exp,
                                     bias=acumT[:, 64 * s + hh:64 * s + hh + 1]),
                                 waits=w, inc="EXD")
                        exd = P.val("EXD")
                        for s in range(2):
                            w = [("EXD", exd)] if s == 0 else []
                            if q >= 1:
                                w.append(("GMD", 2 * (q - 1) + s + 1))
                            P.op("vector", lambda e, s=s, hh=hh, b=par:
                                 e.tensor_scalar(
                                     Mtmp[:, 256 * s:256 * s + 256],
                                     Esb[b][:, 256 * s:256 * s + 256],
                                     1.0,
                                     dtT[:, 64 * s + hh:64 * s + hh + 1],
                                     ALU.min, ALU.mult),
                                 waits=w, inc="MDT")
                            w2 = []
                            if q >= 2 and s == 0:
                                w2.append(("YDD", 2 * (q - 1)))
                            P.op("vector", lambda e, s=s, b=par:
                                 e.tensor_tensor(
                                     MTsb[b][:, 256 * s:256 * s + 256],
                                     Mtmp[:, 256 * s:256 * s + 256],
                                     Gsb[:, 256 * s:256 * s + 256], ALU.mult),
                                 waits=w2, inc="GMD")
                        gmd = P.val("GMD")
                        w = [("CPH", cphv)]
                        if pi >= 1:
                            if h == 0:
                                w.append(("STD", 4 * pi))
                        for s in range(2):
                            seg = 2 * h + s
                            P.op("vector", lambda e, seg=seg, s=s, hh=hh:
                                 e.tensor_scalar_mul(
                                     hdd[:, 64 * seg:64 * seg + 64],
                                     htm[s][:, 64 * hh:64 * hh + 64],
                                     ddT[:, 64 * s + hh:64 * s + hh + 1]),
                                 waits=w if s == 0 else [], inc="HTD")
                        htd = P.val("HTD")
                        w = [("GMD", gmd), ("CPH", cphv)]
                        if pi >= 2 and h == 0:
                            w.append(("U2D", pi - 1))
                        def ydmm(e, h=h, hh=hh, b=par, bp=pi % 2):
                            last = None
                            for s in range(2):
                                last = e.matmul(
                                    p_yd[bp][64 * h:64 * h + 64, 0:256],
                                    htm[s][:, 64 * hh:64 * hh + 64],
                                    MTsb[b][:, 256 * s:256 * s + 256],
                                    start=(s == 0), stop=(s == 1),
                                    skip_group_check=True,
                                    tile_position=(0, 64 * h))
                            return last
                        P.op("tensor", ydmm, waits=w, inc="YDD", n=2)
                    for h in range(2):
                        q = 2 * pi + h
                        w = [("RL", rlbase + (q - q0) + 1)]
                        if pi >= 1 and h == 0:
                            w.append(("U1D", pi))
                        P.op("scalar", lambda e, h=h, b=q % 2: e.activation(
                            sdr[64 * h:64 * h + 64, :],
                            p_rep[b][64 * h:64 * h + 64, 0:256],
                            AF.Exp), waits=w, inc="SDR")
                    sdrv = P.val("SDR")
                    w = [("HTD", P.val("HTD"))]
                    if pi >= 1:
                        w.append(("SEV", pi))
                    def stmm(e):
                        last = None
                        for h in range(2):
                            for s in range(2):
                                seg = 2 * h + s
                                last = e.matmul(
                                    p_st[:, 64 * h:64 * h + 64],
                                    btm[:, 128 * s:128 * s + 128],
                                    hdd[:, 64 * seg:64 * seg + 64],
                                    start=(s == 0), stop=(s == 1),
                                    skip_group_check=True)
                        return last
                    P.op("tensor", stmm, waits=w, inc="STD", n=4)
                    stv = P.val("STD")
                    w = [("STD", stv)]
                    if pi >= 1:
                        w.append(("YOD", 2 * pi))
                    P.op("vector", lambda e: e.tensor_copy(st_sb[:], p_st[:, 0:128]),
                         waits=w, inc="SEV")
                    sev = P.val("SEV")
                    w = [("SEV", sev)]
                    if pi >= 1:
                        w.append(("U1D", pi))
                    def yomm(e, c0=c0):
                        last = None
                        for h in range(2):
                            last = e.matmul(
                                p_yo[64 * h:64 * h + 64, 0:256],
                                st_sb[:, 64 * h:64 * h + 64],
                                bufB[:, 1027 * 33 + c0:1027 * 33 + c0 + CS],
                                start=True, stop=True, skip_group_check=True,
                                tile_position=(0, 64 * h))
                        return last
                    P.op("tensor", yomm, waits=w, inc="YOD", n=2)
                    yod = P.val("YOD")
                    w = [("YOD", yod), ("SDR", sdrv)]
                    if pi >= 1:
                        w.append(("U4D", pi))
                    P.op("vector", lambda e: e.tensor_tensor(
                        u_a[:], p_yo[:, 0:256], sdr[:], ALU.mult),
                        waits=w, inc="U1D")
                    ydd = P.val("YDD")
                    w = [("YDD", ydd)]
                    if pi >= 1:
                        w.append(("U3D", pi))
                    P.op("vector", lambda e, bp=pi % 2: e.tensor_tensor(
                        u_b[:], u_a[:], p_yd[bp][:, 0:256], ALU.add),
                        waits=w, inc="U2D")
                    u2 = P.val("U2D")
                    w = [("U2D", u2)]
                    if pi >= 1:
                        w.append(("U4D", pi))
                    P.op("vector", lambda e, j=j, c0=c0: e.scalar_tensor_tensor(
                        u_a[:], bufB[:, 1027 * j + c0:1027 * j + c0 + CS],
                        dpair[:, j:j + 1], u_b[:], ALU.mult, ALU.add),
                        waits=w, inc="U3D")
                    u3 = P.val("U3D")
                    P.op("gpsimd", lambda e, j=j, c=c, c0=c0: e.tensor_tensor(
                        bufB[:, 1027 * j + c0:1027 * j + c0 + CS],
                        u_a[:], gate_sl(j, CS * c, CS), ALU.mult),
                        waits=[("U3D", u3)], inc="U4D")
                P.barrier()
            if "yf" in debug_taps:
                reg_tap("yf", [128, HBC_C * 1027], BF16)
                P.op("sync", lambda e: e.dma_start(out=taps["yf"], in_=bufB[:]),
                     inc="TAP", n=16)
                P.op("sync", "WAITONLY", waits=[("TAP", P.val("TAP"))])
                P.barrier()

        with ExitStack() as ph:
            def sbp(name, shape, dt=F32):
                return ph.enter_context(nc.sbuf_tensor(name, shape, dt))
            yf2 = sbp("yf2", [128, 1024], BF16)
            row1 = sbp("row1", [1, 512], F32)
            row2 = sbp("row2", [1, 512], F32)
            rrep = sbp("rrep", [128, 1024], F32)
            p_var = ph.enter_context(nc.psum_tensor("p_var", [128, 512], F32))
            p_rr = ph.enter_context(nc.psum_tensor("p_rr", [128, 512], F32))

            for half in range(2):
                for fc in range(IC):
                    w = []
                    if fc >= 2:
                        w.append(("RMM", P.val("RMM") - 1))
                    P.op("scalar", lambda e, fc=fc, half=half: e.activation(
                        yf2[:, 512 * (fc % 2):512 * (fc % 2) + 512],
                        bufB[:, 1027 * fc + HALO + 512 * half:
                             1027 * fc + HALO + 512 * half + 512],
                        AF.Square), waits=w, inc="RSQ")
                    sv = P.val("RSQ")
                    w = [("RSQ", sv)]
                    if half == 1 and fc == 0:
                        w.append(("RRS", 2 * half))
                    P.op("tensor", lambda e, fc=fc: e.matmul(
                        p_var[0:1, :], onesb[:],
                        yf2[:, 512 * (fc % 2):512 * (fc % 2) + 512],
                        start=(fc == 0), stop=(fc == IC - 1),
                        skip_group_check=True),
                        waits=w, inc="RMM")
                mv = P.val("RMM")
                P.op("scalar", lambda e: e.activation(
                    row1[:], p_var[0:1, :], AF.Ln, scale=1.0 / INTER,
                    bias=fcst[0:1, 0:1]),
                    waits=[("RMM", mv)], inc="RRS")
                rr_ = P.val("RRS")
                P.op("scalar", lambda e: e.activation(
                    row2[:], row1[:], AF.Exp, scale=-0.5),
                    waits=[("RRS", rr_)], inc="RRS")
                rv = P.val("RRS")
                w = [("RRS", rv)]
                if half == 1:
                    w.append(("RRC", half))
                P.op("tensor", lambda e: e.matmul(
                    p_rr[:, :], ones1[:], row2[0:1, :], start=True, stop=True,
                    skip_group_check=True),
                    waits=w, inc="RMM")
                mv = P.val("RMM")
                P.op("vector", lambda e, half=half: e.tensor_copy(
                    rrep[:, 512 * half:512 * half + 512], p_rr[:, :]),
                    waits=[("RMM", mv)], inc="RRC")
            rv = P.val("RRC")
            for fc in range(IC):
                P.op("vector", lambda e, fc=fc: e.scalar_tensor_tensor(
                    bufA[:, 1024 * fc:1024 * fc + 1024],
                    bufB[:, 1027 * fc + HALO:1027 * fc + HALO + 1024],
                    normw[:, fc:fc + 1], rrep[:], ALU.mult, ALU.mult),
                    waits=[("RRC", rv)], inc="RYN")
            P.barrier()
            if "yn" in debug_taps:
                reg_tap("yn", [128, GATE_C * 1024], BF16)
                P.op("sync", lambda e: e.dma_start(out=taps["yn"], in_=bufA[:]),
                     inc="TAP", n=16)
                P.op("sync", "WAITONLY", waits=[("TAP", P.val("TAP"))])
                P.barrier()

        with ExitStack() as ph:
            owf = [ph.enter_context(nc.sbuf_tensor(f"owf{i}", [128, 2048], F32))
                  for i in range(2)]
            owb = [ph.enter_context(nc.sbuf_tensor(f"owb{i}", [128, 2048], BF16))
                  for i in range(2)]
            opps = [ph.enter_context(nc.psum_tensor(f"opps{i}", [128, 512], F32))
                   for i in range(4)]
            ui = 0
            for hc in range(DHC):
                for kh in range(2):
                    b = ui % 2
                    w = [("OCA", ui - 1)] if ui >= 2 else []
                    P.op("sync", lambda e, hc=hc, kh=kh, b=b: e.dma_start(
                        out=owf[b][:],
                        in_=d_wout[hc, 16 * kh:16 * kh + 16].rearrange(
                            "k p m -> p k m")),
                        waits=w, inc="ODW", n=16)
                    w = [("ODW", 16 * (ui + 1))]
                    if ui >= 2:
                        w.append(("OMQ", 2 * (ui - 1)))
                    P.op("scalar", lambda e, b=b: e.copy(
                        owb[b][:], owf[b][:]), waits=w, inc="OCA")
                    cav = P.val("OCA")
                    for tt in range(2):
                        w = [("OCA", cav)] if tt == 0 else []
                        if kh == 0 and hc >= 2:
                            w.append(("OEV", 2 * (hc - 1)))
                        slot = (hc % 2) * 2 + tt
                        def omm(e, b=b, kh=kh, tt=tt, slot=slot):
                            last = None
                            for k in range(16):
                                kk = 16 * kh + k
                                last = e.matmul(
                                    opps[slot][:, :],
                                    owb[b][:, 128 * k:128 * k + 128],
                                    bufA[:, 1024 * kk + 512 * tt:
                                         1024 * kk + 512 * tt + 512],
                                    start=(kk == 0), stop=(kk == 31),
                                    skip_group_check=True)
                            return last
                        P.op("tensor", omm, waits=w, inc="OMQ")
                        mqv = P.val("OMQ")
                        if kh == 1:
                            P.op("vector", lambda e, hc=hc, tt=tt, slot=slot:
                                 e.tensor_copy(
                                     bufB[:, 1024 * hc + 512 * tt:
                                          1024 * hc + 512 * tt + 512],
                                     opps[slot][:, :]),
                                 waits=[("OMQ", mqv)], inc="OEV")
                    ui += 1
            P.barrier()
            if "s_attn" in debug_taps:
                reg_tap("s_attn", [128, DHC * 1024], BF16)
                P.op("sync", lambda e: e.dma_start(
                    out=taps["s_attn"], in_=bufB[:, 0:DHC * 1024]), inc="TAP", n=16)
                P.barrier()

        with ExitStack() as ph:
            def sbp(name, shape, dt=F32):
                return ph.enter_context(nc.sbuf_tensor(name, shape, dt))
            w8g = [sbp(f"w8g{i}", [128, 2048], I8) for i in range(2)]
            w8u = [sbp(f"w8u{i}", [128, 2048], I8) for i in range(2)]
            wbg = [sbp(f"wbg{i}", [128, 2048], BF16) for i in range(2)]
            wbu = [sbp(f"wbu{i}", [128, 2048], BF16) for i in range(2)]
            gb = sbp("gb", [128, 512], BF16)
            ub = sbp("ub", [128, 512], BF16)
            p_g = ph.enter_context(nc.psum_tensor("p_g", [128, 512], F32))
            p_u = ph.enter_context(nc.psum_tensor("p_u", [128, 512], F32))
            p_d = [ph.enter_context(nc.psum_tensor(f"p_d{i}", [128, 512], F32))
                   for i in range(2)]

            for half in range(2):
                for ic in range(FIC):
                    ui = half * FIC + ic
                    b = ui % 2
                    w = []
                    if ui >= 2:
                        w.append(("FCA", 2 * ui - 2))
                    if half == 1 and ic < 2:
                        w.append(("GCA", 64))
                    P.op("sync", lambda e, ic=ic, b=b: e.dma_start(
                        out=w8g[b][:], in_=d_wg[ic].rearrange("k p m -> p k m")),
                        waits=w, inc="FDW", n=16)
                    P.op("sync", lambda e, ic=ic, b=b: e.dma_start(
                        out=w8u[b][:], in_=d_wu[ic].rearrange("k p m -> p k m")),
                        inc="FDW", n=16)
                    dwv = P.val("FDW")
                    w = [("FDW", dwv)]
                    if ui >= 2:
                        w.append(("FMQ", 2 * (ui - 1)))
                    P.op("scalar", lambda e, b=b: e.copy(wbg[b][:], w8g[b][:]),
                         waits=w, inc="FCA")
                    P.op("scalar", lambda e, b=b: e.copy(wbu[b][:], w8u[b][:]),
                         inc="FCA")
                    cav = P.val("FCA")
                    w = [("FCA", cav)]
                    if ui >= 1:
                        w.append(("FEV", 3 * ui))
                    def gmm2(e, b=b, half=half):
                        last = None
                        for k in range(EC):
                            last = e.matmul(p_g[:, :], wbg[b][:, 128 * k:128 * k + 128],
                                            xn_sl(k, 512 * half, 512),
                                            start=(k == 0), stop=(k == EC - 1))
                        return last
                    P.op("tensor", gmm2, waits=w, inc="FMQ")
                    def umm(e, b=b, half=half):
                        last = None
                        for k in range(EC):
                            last = e.matmul(p_u[:, :], wbu[b][:, 128 * k:128 * k + 128],
                                            xn_sl(k, 512 * half, 512),
                                            start=(k == 0), stop=(k == EC - 1))
                        return last
                    P.op("tensor", umm, inc="FMQ")
                    mqv = P.val("FMQ")
                    w = [("FMQ", mqv - 1)]
                    if ui >= 1:
                        w.append(("FEV", 3 * ui))
                    P.op("scalar", lambda e, ic=ic: e.activation(
                        gb[:], p_g[:, :], AF.Silu, scale=sgate[:, ic:ic + 1]),
                        waits=w, inc="FEV")
                    P.op("vector", lambda e, ic=ic: e.tensor_scalar_mul(
                        ub[:], p_u[:, :], sup[:, ic:ic + 1]),
                        waits=[("FMQ", mqv)], inc="FEV")
                    ev = P.val("FEV")
                    w = [("FEV", ev)]
                    if half == 1 and ic == 0:
                        w.append(("GMQ", 64))
                    P.op("vector", lambda e, ic=ic: e.tensor_tensor(
                        bufA[:, 512 * ic:512 * ic + 512], gb[:], ub[:], ALU.mult),
                        waits=w, inc="FEV")
                fev = P.val("FEV")
                for hc in range(DHC):
                    dui = half * DHC + hc
                    for kq in range(4):
                        uu = 4 * dui + kq
                        b = uu % 2
                        w = [("GCA", uu - 1)] if uu >= 2 else []
                        if kq == 0 and hc == 0:
                            w.append(("FEV", fev))
                        P.op("sync", lambda e, hc=hc, kq=kq, b=b: e.dma_start(
                            out=w8g[b][:],
                            in_=d_wd[hc, 16 * kq:16 * kq + 16].rearrange(
                                "k p m -> p k m")),
                            waits=w, inc="GDW", n=16)
                        w = [("GDW", 16 * (uu + 1))]
                        if uu >= 2:
                            w.append(("GMQ", uu - 1))
                        P.op("scalar", lambda e, b=b: e.copy(
                            wbg[b][:], w8g[b][:]), waits=w, inc="GCA")
                        cav = P.val("GCA")
                        w = [("GCA", cav), ("FEV", fev)]
                        if dui >= 2 and kq == 0:
                            w.append(("GEV", dui - 1))
                        def dmm(e, b=b, kq=kq, dui=dui):
                            last = None
                            for k in range(16):
                                kk = 16 * kq + k
                                last = e.matmul(
                                    p_d[dui % 2][:, :],
                                    wbg[b][:, 128 * k:128 * k + 128],
                                    bufA[:, 512 * kk:512 * kk + 512],
                                    start=(kk == 0), stop=(kk == 63),
                                    skip_group_check=True)
                            return last
                        P.op("tensor", dmm, waits=w, inc="GMQ")
                    mqv = P.val("GMQ")
                    P.op("scalar", lambda e, hc=hc, half=half, dui=dui:
                         e.activation(
                             bufB[:, DHC * 1024 + 1024 * hc + 512 * half:
                                  DHC * 1024 + 1024 * hc + 512 * half + 512],
                             p_d[dui % 2][:, :], AF.Silu,
                             scale=sdown[:, hc:hc + 1]),
                         waits=[("GMQ", mqv)], inc="GEV")
            P.barrier()
            if "g_ffn" in debug_taps:
                reg_tap("g_ffn", [128, DHC * 1024], BF16)
                P.op("sync", lambda e: e.dma_start(
                    out=taps["g_ffn"], in_=bufB[:, DHC * 1024:2 * DHC * 1024]),
                    inc="TAP", n=16)
                P.op("sync", "WAITONLY", waits=[("TAP", P.val("TAP"))])
                P.barrier()

        with ExitStack() as ph:
            def sbp(name, shape, dt=F32):
                return ph.enter_context(nc.sbuf_tensor(name, shape, dt))
            xb = [sbp(f"fxb{i}", [128, 512], F32) for i in range(2)]
            ot = [sbp(f"fot{i}", [128, 512], F32) for i in range(2)]
            sgp = [sbp(f"fsg{i}", [128, 128], BF16) for i in range(2)]
            p_t = [ph.enter_context(nc.psum_tensor(f"fpt{i}", [128, 512], BF16))
                   for i in range(2)]
            un = 0
            for tt in range(8):
                for fq in range(4):
                    b = un % 2
                    w = [("FAD", un - 1)] if un >= 2 else []
                    P.op("sync", lambda e, tt=tt, fq=fq, b=b: e.dma_start(
                        out=xb[b][:], in_=dx[128 * tt:128 * tt + 128,
                                            512 * fq:512 * fq + 512]),
                        waits=w, inc="FXD", n=16)
                    for sub in range(4):
                        fc = 4 * fq + sub
                        w = []
                        if sub >= 1:
                            w.append(("FTR", 4 * un + sub))
                        elif un >= 2:
                            w.append(("FTR", 4 * (un - 1)))
                        P.op("vector", lambda e, fc=fc, tt=tt, b=b: e.tensor_tensor(
                            sgp[b][:], bufB[:, 1024 * fc + 128 * tt:
                                            1024 * fc + 128 * tt + 128],
                            bufB[:, DHC * 1024 + 1024 * fc + 128 * tt:
                                 DHC * 1024 + 1024 * fc + 128 * tt + 128],
                            ALU.mult),
                            waits=w, inc="FSG")
                        sgv = P.val("FSG")
                        w = [("FSG", sgv)]
                        if un >= 2 and sub == 0:
                            w.append(("FAD", un - 1))
                        P.op("tensor", lambda e, sub=sub, b=b: e.transpose(
                            p_t[b][:, 128 * sub:128 * sub + 128], sgp[b][:],
                            idb[:]),
                            waits=w, inc="FTR")
                    trv = P.val("FTR")
                    xdv = P.val("FXD")
                    w = [("FTR", trv), ("FXD", xdv)]
                    if un >= 2:
                        w.append(("FOT", 16 * (un - 1)))
                    P.op("vector", lambda e, b=b: e.tensor_tensor(
                        ot[b][:], p_t[b][:, :], xb[b][:], ALU.add),
                        waits=w, inc="FAD")
                    fav = P.val("FAD")
                    P.op("sync", lambda e, tt=tt, fq=fq, b=b: e.dma_start(
                        out=d_out[128 * tt:128 * tt + 128, 512 * fq:512 * fq + 512],
                        in_=ot[b][:]),
                        waits=[("FAD", fav)], inc="FOT", n=16)
                    un += 1
            P.op("sync", "WAITONLY", waits=[("FOT", P.val("FOT"))])
            if taps:
                P.op("sync", "WAITONLY", waits=[("TAP", P.val("TAP"))])

        with nc.Block() as block:
            P.emit(block)

    return nc, sorted(taps.keys())



def _tile_lhsT(w):
    OUT, IN = w.shape
    OC, KC = OUT // 128, IN // 128
    return np.ascontiguousarray(w.reshape(OC, 128, KC, 128).transpose(0, 2, 3, 1))


def _tile_rhsT(w):
    IN, OUT = w.shape
    KC, OC = IN // 128, OUT // 128
    return np.ascontiguousarray(w.reshape(KC, 128, OC, 128).transpose(2, 0, 1, 3))


def _host_prep(inputs):
    W_in = np.asarray(inputs["L0_attn_W_in"], np.float32)
    W_out = np.asarray(inputs["L0_attn_W_out"], np.float32)
    wg = np.asarray(inputs["L0_ffn_w_gate"], np.int8)
    wu = np.asarray(inputs["L0_ffn_w_up"], np.int8)
    wd = np.asarray(inputs["L0_ffn_w_down"], np.int8)
    convw = np.asarray(inputs["L0_attn_conv_w"], np.float32)
    convb = np.asarray(inputs["L0_attn_conv_b"], np.float32)
    normw = np.asarray(inputs["L0_attn_norm_w"], np.float32)
    s_g = np.asarray(inputs["L0_ffn_s_gate"], np.float32)
    s_u = np.asarray(inputs["L0_ffn_s_up"], np.float32)
    s_d = np.asarray(inputs["L0_ffn_s_down"], np.float32)
    dtbv = np.asarray(inputs["L0_attn_dt_bias"], np.float32)
    alogv = np.asarray(inputs["L0_attn_A_log"], np.float32)
    D = np.asarray(inputs["L0_attn_D"], np.float32)

    W_in_pad = np.zeros((PC * 128, E), np.float32)
    W_in_pad[:PROJ] = W_in
    d = {}
    d["w_in_img"] = _tile_lhsT(W_in_pad)
    d["w_out_img"] = _tile_lhsT(W_out)
    d["wg_img"] = _tile_rhsT(wg)
    d["wu_img"] = _tile_rhsT(wu)
    d["wd_img"] = _tile_rhsT(wd)
    d["convw"] = np.ascontiguousarray(
        convw.reshape(HBC_C, 128, 4).transpose(1, 0, 2).reshape(128, HBC_C * 4))
    d["convb"] = np.ascontiguousarray(convb.reshape(HBC_C, 128).T)
    d["normw"] = np.ascontiguousarray(normw.reshape(IC, 128).T)
    d["sgate"] = np.ascontiguousarray(s_g.reshape(FIC, 128).T)
    d["sup"] = np.ascontiguousarray(s_u.reshape(FIC, 128).T)
    d["sdown"] = np.ascontiguousarray(s_d.reshape(DHC, 128).T)
    d["dtb"] = np.ascontiguousarray(dtbv.reshape(NH, 1))
    d["alog"] = np.ascontiguousarray(alogv.reshape(NH, 1))
    p = np.arange(128)[:, None]
    jj = np.arange(IC)[None, :]
    d["dpair"] = np.ascontiguousarray(D[2 * jj + p // 64].astype(np.float32))
    d["idf"] = np.eye(128, dtype=np.float32)
    d["idb"] = np.eye(128, dtype=np.float32).astype(BF)
    s_i = np.arange(128)[:, None]
    l_i = np.arange(256)[None, :]
    m0 = (s_i <= l_i).astype(np.float32)
    m1 = (s_i + 128 <= l_i).astype(np.float32)
    d["gmask"] = np.ascontiguousarray(np.concatenate([m0, m1], 1).astype(BF))
    fc = np.zeros((128, 2), np.float32); fc[:, 0] = EPS; fc[:, 1] = 1.0
    d["fconst"] = fc
    return d


def _in_maps(inputs, shared=None):
    X = np.asarray(inputs["L0_attn_input_states"], np.float32)
    if shared is None:
        shared = _host_prep(inputs)
    maps = []
    for c in range(8):
        b, h = c // 2, c % 2
        m = dict(shared)
        m["x"] = np.ascontiguousarray(X[b, 1024 * h:1024 * h + 1024])
        if h == 0:
            m["xh"] = np.zeros((HALO, E), np.float32)
        else:
            m["xh"] = np.ascontiguousarray(X[b, 1021:1024])
        maps.append(m)
    return maps


def kernel(**inputs):
    _ = inputs.get("L0_attn_mask")
    nc, _unused = build_nc()
    in_maps = _in_maps(inputs)
    res = run_bass_kernel_spmd(nc, in_maps, core_ids=list(range(8)))
    out = np.empty((4, 2048, E), np.float32)
    for c in range(8):
        b, h = c // 2, c % 2
        out[b, 1024 * h:1024 * h + 1024] = res.results[c]["out"]
    return out


# version 18
# speedup vs baseline: 1.1628x; 1.1200x over previous
import numpy as np
import ml_dtypes
from contextlib import ExitStack

import concourse.bass as bass
import concourse.mybir as mybir
from concourse.bass_utils import run_bass_kernel_spmd

F32 = mybir.dt.float32
BF16 = mybir.dt.bfloat16
I8 = mybir.dt.int8
AF = mybir.ActivationFunctionType
ALU = mybir.AluOpType
AXX = mybir.AxisListType.X

BF = ml_dtypes.bfloat16

E = 2048
T = 1024
HALO = 3
NH = 64
CS = 256
NCH = T // CS
PROJ = 8512
PC = 67
GATE_C = 32
HBC_C = 34
INTER = 4096
IC = 32
FI = 8192
FIC = 64
DHC = 16
EC = 16
EPS = 1e-5


class Op:
    __slots__ = ("eng", "fn", "waits", "incs")
    def __init__(self, eng, fn, waits, incs):
        self.eng, self.fn, self.waits, self.incs = eng, fn, waits, incs


class Prog:
    ENGINES = ("sync", "tensor", "vector", "scalar", "gpsimd")

    def __init__(self, nc, es):
        self.nc = nc
        self.ops = []
        self.sems = {}
        self.cnt = {}
        self.es = es

    def sem(self, name):
        if name not in self.sems:
            self.sems[name] = self.es.enter_context(self.nc.semaphore(name))
            self.cnt[name] = 0
        return self.sems[name]

    def val(self, name):
        self.sem(name)
        return self.cnt.get(name, 0)

    def op(self, eng, fn, waits=(), inc=None, n=1):
        incs = []
        if inc is not None:
            self.sem(inc)
            self.cnt[inc] += n
            incs = [(inc, n)]
        ws = []
        for s, v in waits:
            self.sem(s)
            if v > 0:
                ws.append((s, v))
        self.ops.append(Op(eng, fn, ws, incs))

    def barrier(self):
        self.sem("PH")
        for e in self.ENGINES:
            self.cnt["PH"] += 1
            fn = None if e == "sync" else "DRAIN"
            self.ops.append(Op(e, fn, [], [("PH", 1)]))
        v = self.cnt["PH"]
        for e in self.ENGINES:
            self.ops.append(Op(e, "WAITONLY", [("PH", v)], []))

    def replay(self, engname, e):
        for o in self.ops:
            if o.eng != engname:
                continue
            for s, v in o.waits:
                e.wait_ge(self.sems[s], v)
            if o.fn == "WAITONLY":
                continue
            if o.fn == "DRAIN":
                inst = e.drain()
            else:
                inst = e.nop() if o.fn is None else o.fn(e)
            for s, n in o.incs:
                if inst is None:
                    inst = e.nop()
                inst.then_inc(self.sems[s], n)

    def emit(self, block):
        block.sync(lambda e: self.replay("sync", e))
        block.tensor(lambda e: self.replay("tensor", e))
        block.vector(lambda e: self.replay("vector", e))
        block.scalar(lambda e: self.replay("scalar", e))
        block.gpsimd(lambda e: self.replay("gpsimd", e))


def build_nc(debug_taps=()):
    nc = bass.Bass("TRN2", target_bir_lowering=False, debug=False)

    dx = nc.dram_tensor("x", [T, E], F32, kind="ExternalInput").ap()
    dxh = nc.dram_tensor("xh", [HALO, E], F32, kind="ExternalInput").ap()
    d_win = nc.dram_tensor("w_in_img", [PC, 16, 128, 128], F32, kind="ExternalInput").ap()
    d_wout = nc.dram_tensor("w_out_img", [DHC, 32, 128, 128], F32, kind="ExternalInput").ap()
    d_wg = nc.dram_tensor("wg_img", [FIC, 16, 128, 128], I8, kind="ExternalInput").ap()
    d_wu = nc.dram_tensor("wu_img", [FIC, 16, 128, 128], I8, kind="ExternalInput").ap()
    d_wd = nc.dram_tensor("wd_img", [DHC, 64, 128, 128], I8, kind="ExternalInput").ap()
    d_convw = nc.dram_tensor("convw", [128, HBC_C * 4], F32, kind="ExternalInput").ap()
    d_convb = nc.dram_tensor("convb", [128, HBC_C], F32, kind="ExternalInput").ap()
    d_normw = nc.dram_tensor("normw", [128, IC], F32, kind="ExternalInput").ap()
    d_sg = nc.dram_tensor("sgate", [128, FIC], F32, kind="ExternalInput").ap()
    d_su = nc.dram_tensor("sup", [128, FIC], F32, kind="ExternalInput").ap()
    d_sd = nc.dram_tensor("sdown", [128, DHC], F32, kind="ExternalInput").ap()
    d_dtb = nc.dram_tensor("dtb", [NH, 1], F32, kind="ExternalInput").ap()
    d_alog = nc.dram_tensor("alog", [NH, 1], F32, kind="ExternalInput").ap()
    d_dpair = nc.dram_tensor("dpair", [128, IC], F32, kind="ExternalInput").ap()
    d_idf = nc.dram_tensor("idf", [128, 128], F32, kind="ExternalInput").ap()
    d_idb = nc.dram_tensor("idb", [128, 128], BF16, kind="ExternalInput").ap()
    d_gmask = nc.dram_tensor("gmask", [128, 512], BF16, kind="ExternalInput").ap()
    d_fc = nc.dram_tensor("fconst", [128, 2], F32, kind="ExternalInput").ap()
    d_out = nc.dram_tensor("out", [T, E], F32, kind="ExternalOutput").ap()

    taps = {}
    def reg_tap(name, shape, dt):
        taps[name] = nc.dram_tensor("tap_" + name, shape, dt, kind="ExternalOutput").ap()

    with ExitStack() as es:
        P = Prog(nc, es)

        def sb(name, shape, dt=F32):
            return es.enter_context(nc.sbuf_tensor(name, shape, dt))

        xn = sb("xn", [128, EC * 1024], BF16)
        xnh = sb("xnh", [128, EC * HALO], BF16)
        bufA = sb("bufA", [128, GATE_C * 1024], BF16)
        bufB = sb("bufB", [128, HBC_C * 1027], BF16)
        dt_raw = sb("dt_raw", [NH, T], F32)
        convw = sb("convw_s", [128, HBC_C * 4], F32)
        convb = sb("convb_s", [128, HBC_C], F32)
        normw = sb("normw_s", [128, IC], F32)
        sgate = sb("sgate_s", [128, FIC], F32)
        sup = sb("sup_s", [128, FIC], F32)
        sdown = sb("sdown_s", [128, DHC], F32)
        dtb = sb("dtb_s", [NH, 1], F32)
        alog = sb("alog_s", [NH, 1], F32)
        dpair = sb("dpair_s", [128, IC], F32)
        idf = sb("idf_s", [128, 128], F32)
        idb = sb("idb_s", [128, 128], BF16)
        gmask = sb("gmask_s", [128, 512], BF16)
        negA = sb("negA", [NH, 1], F32)
        expa = sb("expa", [NH, 1], F32)
        ones1 = sb("ones1", [1, 128], F32)
        onesb = sb("onesb", [128, 1], BF16)
        zer_c = sb("zer_c", [NH, CS], F32)
        fcst = sb("fcst", [128, 2], F32)

        def xn_sl(k, c0, w):
            return xn[:, 1024 * k + c0: 1024 * k + c0 + w]
        def xnh_sl(k):
            return xnh[:, HALO * k: HALO * k + HALO]
        def hbc_sl(cc, c0, w):
            return bufB[:, 1027 * cc + c0: 1027 * cc + c0 + w]
        def gate_sl(j, c0, w):
            return bufA[:, 1024 * j + c0: 1024 * j + c0 + w]

        for dst, src in [(convw, d_convw), (convb, d_convb), (normw, d_normw),
                         (sgate, d_sg), (sup, d_su), (sdown, d_sd),
                         (dtb, d_dtb), (alog, d_alog), (dpair, d_dpair),
                         (idf, d_idf), (idb, d_idb), (gmask, d_gmask),
                         (fcst, d_fc)]:
            P.op("sync", lambda e, d=dst, s=src: e.dma_start(out=d[:], in_=s),
                 inc="DC", n=16)
        dcv = P.val("DC")
        P.op("vector", lambda e: e.memset(ones1[:], 1.0))
        P.op("vector", lambda e: e.memset(onesb[:], 1.0))
        P.op("vector", lambda e: e.memset(zer_c[:], 0.0))
        P.op("scalar", lambda e: e.activation(expa[:], alog[:], AF.Exp),
             waits=[("DC", dcv)], inc="INI")
        P.op("vector", lambda e: e.tensor_scalar_mul(negA[:], expa[:], -1.0),
             waits=[("INI", 1)], inc="INI")

        with ExitStack() as ph:
            def sbp(name, shape, dt=F32):
                return ph.enter_context(nc.sbuf_tensor(name, shape, dt))
            xt = [sbp(f"xt{i}", [128, E], F32) for i in range(2)]
            xht = sbp("xht", [128, E], F32)
            sqt_t = xht[:, 0:E // 2].bitcast(BF16)
            sqt_h = xt[0][:, 0:E // 2].bitcast(BF16)
            s1_ = [sbp(f"s1{i}", [128, 1], F32) for i in range(2)]
            ssq_ = [sbp(f"ssq{i}", [128, 1], F32) for i in range(2)]
            mcol_ = [sbp(f"mcol{i}", [128, 1], F32) for i in range(2)]
            mm2_ = [sbp(f"mm2{i}", [128, 1], F32) for i in range(2)]
            vcol_ = [sbp(f"vcol{i}", [128, 1], F32) for i in range(2)]
            rstd_ = [sbp(f"rstd{i}", [128, 1], F32) for i in range(2)]
            mb_ = [sbp(f"mb{i}", [128, 1], F32) for i in range(2)]
            xn_tm_ = [sbp(f"xn_tm{i}", [128, E], BF16) for i in range(2)]
            stt_dump = sbp("stt_dump", [128, 7], F32)
            p_tr = ph.enter_context(nc.psum_tensor("p_tr", [128, 1024], BF16))

            def ln_head(i, is_halo):
                p2 = i % 2
                s1, ssq = s1_[p2], ssq_[p2]
                buf = xht if is_halo else xt[i % 2]
                src = dxh if is_halo else dx[128 * i:128 * i + 128, :]
                bb = buf[:]
                npart = 128
                w = [("LNS", 3 * (i - 2) + 3)] if (not is_halo and i >= 2) else []
                if is_halo:
                    P.op("vector", lambda e: e.memset(xht[:], 0.0),
                         waits=[("LNSQ", 8)], inc="LNH")
                    P.op("sync", lambda e: e.dma_start(out=xht[0:HALO, :], in_=src),
                         waits=[("LNH", 1)], inc="DX", n=16)
                else:
                    P.op("sync", lambda e: e.dma_start(out=bb, in_=src),
                         waits=w, inc="DX", n=16)
                w = [("DX", 16 * (i + 1))]
                if not is_halo and i == 0:
                    w.append(("DC", P.val("DC")))
                if i >= 2:
                    w.append(("LNV", 4 * (i - 2) + 3))
                sqt = sqt_h if is_halo else sqt_t
                P.op("scalar", lambda e, sqt=sqt: e.activation(
                    sqt[0:npart, :], bb, AF.Square, accum_out=ssq[0:npart, :]),
                    waits=w, inc="LNSQ")
                P.op("vector", lambda e: e.reduce_sum(s1[0:npart, :], bb, axis=AXX),
                     waits=[("DX", 16 * (i + 1))], inc="LNRD")

            def ln_tile(i, is_halo):
                p2 = i % 2
                s1, ssq, mcol, mm2 = s1_[p2], ssq_[p2], mcol_[p2], mm2_[p2]
                vcol, rstd, mb, xn_tm = (vcol_[p2], rstd_[p2], mb_[p2],
                                         xn_tm_[p2])
                buf = xht if is_halo else xt[i % 2]
                npart = 128
                bb = buf[:]
                P.op("vector", lambda e: e.tensor_scalar_mul(
                    mcol[0:npart, :], s1[0:npart, :], 1.0 / E),
                    waits=[("LNRD", i + 1)], inc="LNV")
                P.op("vector", lambda e: e.tensor_mul(
                    mm2[0:npart, :], mcol[0:npart, :], mcol[0:npart, :]),
                    waits=[("LNV", 4 * i + 1)], inc="LNV")
                P.op("vector", lambda e: e.scalar_tensor_tensor(
                    vcol[0:npart, :], ssq[0:npart, :], 1.0 / E, mm2[0:npart, :],
                    ALU.mult, ALU.subtract),
                    waits=[("LNSQ", i + 1), ("LNV", 4 * i + 2)], inc="LNV")
                P.op("scalar", lambda e: e.activation(
                    rstd[0:npart, :], vcol[0:npart, :], AF.Ln,
                    bias=fcst[0:npart, 0:1]),
                    waits=[("LNV", 4 * i + 3)], inc="LNS")
                P.op("scalar", lambda e: e.activation(
                    rstd[0:npart, :], rstd[0:npart, :], AF.Exp, scale=-0.5),
                    waits=[("LNS", 3 * i + 1)], inc="LNS")
                P.op("vector", lambda e: e.scalar_tensor_tensor(
                    mb[0:npart, :], mcol[0:npart, :], -1.0, rstd[0:npart, :],
                    ALU.mult, ALU.mult),
                    waits=[("LNS", 3 * i + 2)], inc="LNV")
                w = [("LNV", 4 * i + 4)]
                if i >= 2:
                    w.append(("LNP", 16 * (i - 1)))
                P.op("scalar", lambda e: e.activation(
                    xn_tm[0:npart, :], bb, AF.Identity, bias=mb[0:npart, :],
                    scale=rstd[0:npart, :]),
                    waits=w, inc="LNS")
                sv = 3 * i + 3
                if is_halo:
                    for half in range(2):
                        for k8 in range(8):
                            k = 8 * half + k8
                            w = []
                            if k == 0:
                                w.append(("LNS", sv))
                            if k8 == 0:
                                w.append(("LNT", 2 * i + half))
                            P.op("tensor", lambda e, k=k, k8=k8: e.transpose(
                                p_tr[:, 128 * k8:128 * k8 + 128],
                                xn_tm[:, 128 * k:128 * k + 128], idb[:]),
                                waits=w, inc="LNP")
                        P.op("vector", lambda e, half=half: e.tensor_copy(
                            xnh[:, :].rearrange("p (k c) -> p k c", c=HALO)
                            [:, 8 * half:8 * half + 8, :],
                            p_tr[:, 0:1024].rearrange("p (k c) -> p k c", c=128)
                            [:, :, 0:HALO]),
                            waits=[("LNP", 16 * i + 8 * (half + 1))], inc="LNT")
                else:
                    for half in range(2):
                        for k8 in range(8):
                            k = 8 * half + k8
                            w = []
                            if k == 0:
                                w.append(("LNS", sv))
                            if k8 == 0:
                                w.append(("LNT", 2 * i + half))
                            P.op("tensor", lambda e, k=k, k8=k8: e.transpose(
                                p_tr[:, 128 * k8:128 * k8 + 128],
                                xn_tm[:, 128 * k:128 * k + 128], idb[:]),
                                waits=w, inc="LNP")
                        P.op("vector", lambda e, i=i, half=half: e.tensor_copy(
                            xn[:, :].rearrange("p (k t) -> p k t", t=1024)
                              [:, 8 * half:8 * half + 8, 128 * i:128 * i + 128],
                            p_tr[:, 0:1024].rearrange("p (k t) -> p k t", t=128)),
                            waits=[("LNP", 16 * i + 8 * (half + 1))], inc="LNT")

            ln_head(0, False)
            ln_head(1, False)
            for i in range(8):
                ln_tile(i, False)
                if i + 2 < 8:
                    ln_head(i + 2, False)
                elif i + 2 == 8:
                    ln_head(8, True)
            ln_tile(8, True)
            P.barrier()
            if "lnh" in debug_taps:
                reg_tap("lnh_xt", [HALO, E], F32)
                reg_tap("lnh_xntm", [HALO, E], BF16)
                reg_tap("lnh_st", [128, 7], F32)
                for ci, st in enumerate([s1, ssq, mcol, mm2, vcol, rstd, mb]):
                    P.op("vector", lambda e, ci=ci, st=st: e.tensor_copy(
                        stt_dump[:, ci:ci + 1], st[:]), inc="TAP0")
                P.op("sync", lambda e: e.dma_start(out=taps["lnh_xt"], in_=xht[0:HALO, :]),
                     waits=[("TAP0", 7)], inc="TAP", n=16)
                P.op("sync", lambda e: e.dma_start(out=taps["lnh_xntm"],
                     in_=xn_tm[0:HALO, :]), inc="TAP", n=16)
                P.op("sync", lambda e: e.dma_start(out=taps["lnh_st"],
                     in_=stt_dump[:]), inc="TAP", n=16)
                P.barrier()
            if "xn" in debug_taps:
                reg_tap("xn", [128, EC * 1024], BF16)
                reg_tap("xnh", [128, EC * HALO], BF16)
                P.op("sync", lambda e: e.dma_start(out=taps["xn"], in_=xn[:]),
                     inc="TAP", n=16)
                P.op("sync", lambda e: e.dma_start(out=taps["xnh"], in_=xnh[:]),
                     inc="TAP", n=16)
                P.op("sync", "WAITONLY", waits=[("TAP", P.val("TAP"))])
                P.barrier()

        with ExitStack() as ph:
            wf = [ph.enter_context(nc.sbuf_tensor(f"wf{i}", [128, 2048], F32))
                  for i in range(2)]
            wb = [ph.enter_context(nc.sbuf_tensor(f"wb{i}", [128, 2048], BF16))
                  for i in range(2)]
            pps = [ph.enter_context(nc.psum_tensor(f"pps{i}", [128, 512], F32))
                   for i in range(4)]
            acc = [ph.enter_context(nc.sbuf_tensor(f"acc{i}", [128, T], F32))
                   for i in range(2)]

            def emit_conv(cc):
                a0, a1 = acc[0], acc[1]
                w = [("PEV", 3 * (cc + 1))]
                if cc >= 1:
                    w.append(("CVS", cc))
                P.op("vector", lambda e, cc=cc, a0=a0: e.tensor_scalar(
                    a0[:], hbc_sl(cc, 0, T), convw[:, 4 * cc:4 * cc + 1],
                    convb[:, cc:cc + 1], ALU.mult, ALU.add),
                    waits=w, inc="CV0")
                v0 = P.val("CV0")
                P.op("vector", lambda e, cc=cc, a0=a0, a1=a1: e.scalar_tensor_tensor(
                    a1[:], hbc_sl(cc, 1, T), convw[:, 4 * cc + 1:4 * cc + 2], a0[:],
                    ALU.mult, ALU.add),
                    waits=[("CV0", v0)], inc="CV1")
                v1 = P.val("CV1")
                P.op("vector", lambda e, cc=cc, a0=a0, a1=a1: e.scalar_tensor_tensor(
                    a0[:], hbc_sl(cc, 2, T), convw[:, 4 * cc + 2:4 * cc + 3], a1[:],
                    ALU.mult, ALU.add),
                    waits=[("CV1", v1)], inc="CV2")
                v2 = P.val("CV2")
                P.op("vector", lambda e, cc=cc, a0=a0, a1=a1: e.scalar_tensor_tensor(
                    a1[:], hbc_sl(cc, 3, T), convw[:, 4 * cc + 3:4 * cc + 4], a0[:],
                    ALU.mult, ALU.add),
                    waits=[("CV2", v2)], inc="CV3")
                v3 = P.val("CV3")
                P.op("scalar", lambda e, cc=cc, a1=a1: e.activation(
                    hbc_sl(cc, HALO, T), a1[:], AF.Silu),
                    waits=[("CV3", v3)], inc="CVS")

            order = list(range(32, 67)) + list(range(32))
            cumg = [0]
            for fc_ in order:
                cumg.append(cumg[-1] + (3 if 32 <= fc_ <= 65 else 2))
            grp = 0

            def proj_fetch(oi):
                fc = order[oi]
                w = [("PCA", oi - 1)] if oi >= 2 else []
                P.op("sync", lambda e, fc=fc, b=oi % 2: e.dma_start(
                    out=wf[b][:], in_=d_win[fc].rearrange("k p m -> p k m")),
                    waits=w, inc="PDW", n=16)
                w = [("PDW", 16 * (oi + 1))]
                if oi >= 2:
                    w.append(("PMQ", cumg[oi - 1]))
                P.op("scalar", lambda e, b=oi % 2: e.copy(wb[b][:], wf[b][:]),
                     waits=w, inc="PCA")

            proj_fetch(0)
            for oi, fc in enumerate(order):
                if oi + 1 < len(order):
                    proj_fetch(oi + 1)
                cav = oi + 1
                groups = [(HALO, 512), (HALO + 512, 512)]
                if 32 <= fc <= 65:
                    groups.append((0, HALO))
                for gi, (c0, wdt) in enumerate(groups):
                    slot = grp % 4
                    wts = [("PCA", cav)] if gi == 0 else []
                    if grp >= 4:
                        wts.append(("PEV", grp - 3))
                    def mmg(e, b=oi % 2, c0=c0, wdt=wdt, slot=slot):
                        last = None
                        for k in range(EC):
                            rhs = xnh_sl(k) if c0 == 0 else xn_sl(k, c0 - HALO, wdt)
                            last = e.matmul(pps[slot][:, 0:wdt],
                                            wb[b][:, 128 * k:128 * k + 128],
                                            rhs, start=(k == 0), stop=(k == EC - 1))
                        return last
                    P.op("tensor", mmg, waits=wts, inc="PMQ")
                    grp += 1
                    mqv = P.val("PMQ")
                    if fc < GATE_C:
                        P.op("scalar", lambda e, fc=fc, c0=c0, wdt=wdt, slot=slot:
                             e.activation(gate_sl(fc, c0 - HALO, wdt),
                                          pps[slot][:, 0:wdt], AF.Silu),
                             waits=[("PMQ", mqv)], inc="PEV")
                    elif fc < 66:
                        P.op("vector", lambda e, cc=fc - 32, c0=c0, wdt=wdt, slot=slot:
                             e.tensor_copy(hbc_sl(cc, c0, wdt), pps[slot][:, 0:wdt]),
                             waits=[("PMQ", mqv)], inc="PEV")
                    else:
                        P.op("vector", lambda e, c0=c0, wdt=wdt, slot=slot:
                             e.tensor_copy(dt_raw[:, c0 - HALO:c0 - HALO + wdt],
                                           pps[slot][0:NH, 0:wdt]),
                             waits=[("PMQ", mqv)], inc="PEV")
                if 35 <= oi <= 65:
                    emit_conv(oi - 35)
                elif oi == 66:
                    for cc_ in range(31, HBC_C):
                        emit_conv(cc_)
            P.barrier()
            if "hbc_raw" in debug_taps:
                reg_tap("hbc_raw", [128, HBC_C * 1027], BF16)
                P.op("sync", lambda e: e.dma_start(out=taps["hbc_raw"], in_=bufB[:]),
                     inc="TAP", n=16)
            if "gate" in debug_taps:
                reg_tap("gate", [128, GATE_C * 1024], BF16)
                P.op("sync", lambda e: e.dma_start(out=taps["gate"], in_=bufA[:]),
                     inc="TAP", n=16)
            if "dt_raw" in debug_taps:
                reg_tap("dt_raw", [NH, T], F32)
                P.op("sync", lambda e: e.dma_start(out=taps["dt_raw"], in_=dt_raw[:]),
                     inc="TAP", n=16)
            if debug_taps:
                P.op("sync", "WAITONLY", waits=[("TAP", P.val("TAP"))])
                P.barrier()

        with ExitStack() as ph:
            def sbp(name, shape, dt=F32):
                return ph.enter_context(nc.sbuf_tensor(name, shape, dt))
            htm = [sbp(f"htm{s}", [128, INTER], BF16) for s in range(2)]
            acumT = sbp("acumT", [128, 2 * 64], F32)
            dtT = sbp("dtT", [128, 2 * 64], F32)
            ddT = sbp("ddT", [128, 2 * 64], F32)
            btm = sbp("btm", [128, 2 * 128], BF16)
            dt_c = sbp("dt_c", [NH, CS], F32)
            ac_c = sbp("ac_c", [NH, CS], F32)
            sp3 = sbp("sp3", [NH, CS], F32)
            Esb = [sbp(f"Esb{i}", [128, 512], BF16) for i in range(2)]
            MTsb = [sbp(f"MTsb{i}", [128, 512], BF16) for i in range(2)]
            Mtmp = sbp("Mtmp", [128, 512], BF16)
            Gsb = sbp("Gsb", [128, 512], BF16)
            sdr = sbp("sdr", [128, CS], BF16)
            st_sb = sbp("st_sb", [128, 128], BF16)
            hdd = sbp("hdd", [128, 4 * 64], BF16)
            u_a = sbp("u_a", [128, CS], F32)
            u_b = sbp("u_b", [128, CS], F32)
            sp1 = u_a[0:NH, :]
            sp2 = u_b[0:NH, :]
            dec_c = sp1
            dd_c = sp2
            rows = [sbp(f"rows{i}", [1, CS], F32) for i in range(3)]
            p_rep = [ph.enter_context(nc.psum_tensor(f"p_rep{i}", [128, 512], F32))
                     for i in range(2)]
            p_yd = [ph.enter_context(nc.psum_tensor(f"p_yd{i}", [128, 512], F32))
                    for i in range(2)]
            p_yo = ph.enter_context(nc.psum_tensor("p_yo", [128, 512], F32))
            p_st = ph.enter_context(nc.psum_tensor("p_st", [128, 512], F32))
            p_tb = ph.enter_context(nc.psum_tensor("p_tb", [128, 1024], BF16))
            p_tf = ph.enter_context(nc.psum_tensor("p_tf", [128, 512], F32))

            for c in range(NCH):
                c0 = HALO + CS * c
                P.op("vector", lambda e, c=c: e.tensor_scalar_add(
                    sp1, dt_raw[:, CS * c:CS * c + CS], dtb[:]), inc="CP")
                v = P.val("CP")
                P.op("vector", lambda e: e.scalar_tensor_tensor(
                    sp2, sp1, -1.0, sp1, ALU.mult, ALU.max),
                    waits=[("CP", v)], inc="CP")
                v = P.val("CP")
                P.op("scalar", lambda e: e.activation(sp3[:], sp2, AF.Exp,
                                                      scale=-1.0),
                     waits=[("CP", v)], inc="CP")
                v = P.val("CP")
                P.op("scalar", lambda e: e.activation(sp2, sp3[:], AF.Ln,
                                                      bias=fcst[0:NH, 1:2]),
                     waits=[("CP", v)], inc="CP")
                v = P.val("CP")
                P.op("vector", lambda e: e.tensor_scalar_max(sp3[:], sp1, 0.0),
                     waits=[("CP", v)], inc="CP")
                v = P.val("CP")
                P.op("vector", lambda e: e.tensor_add(dt_c[:], sp3[:], sp2),
                     waits=[("CP", v)], inc="CP")
                v = P.val("CP")
                P.op("vector", lambda e: e.tensor_scalar_mul(sp1, dt_c[:],
                                                             negA[:]),
                     waits=[("CP", v)], inc="CP")
                v = P.val("CP")
                P.op("vector", lambda e: e.tensor_tensor_scan(
                    ac_c[:], sp1, zer_c[:], 0.0, ALU.add, ALU.add),
                    waits=[("CP", v)], inc="CP")
                v = P.val("CP")
                P.op("scalar", lambda e: e.activation(
                    dec_c, ac_c[:], AF.Exp, bias=ac_c[:, CS - 1:CS],
                    scale=-1.0), waits=[("CP", v)], inc="CP")
                v = P.val("CP")
                P.op("vector", lambda e: e.tensor_mul(dd_c, dt_c[:], dec_c),
                     waits=[("CP", v)], inc="CP")
                v = P.val("CP")
                for s in range(2):
                    P.op("tensor", lambda e, s=s: e.transpose(
                        p_tf[:, 64 * s:64 * s + 64],
                        ac_c[:, 128 * s:128 * s + 128], idf[0:NH, 0:NH]),
                        waits=[("CP", v)] if s == 0 else [], inc="CPT")
                    P.op("tensor", lambda e, s=s: e.transpose(
                        p_tf[:, 128 + 64 * s:128 + 64 * s + 64],
                        dt_c[:, 128 * s:128 * s + 128], idf[0:NH, 0:NH]),
                        inc="CPT")
                    P.op("tensor", lambda e, s=s: e.transpose(
                        p_tf[:, 256 + 64 * s:256 + 64 * s + 64],
                        dd_c[:, 128 * s:128 * s + 128], idf[0:NH, 0:NH]),
                        inc="CPT")
                tv = P.val("CPT")
                P.op("vector", lambda e: e.tensor_scalar_mul(
                    acumT[:], p_tf[:, 0:128], -1.0),
                     waits=[("CPT", tv)], inc="CP")
                P.op("vector", lambda e: e.tensor_copy(dtT[:], p_tf[:, 128:256]),
                     inc="CP")
                P.op("vector", lambda e: e.tensor_copy(ddT[:], p_tf[:, 256:384]),
                     inc="CP")
                cpv = P.val("CP")
                for s in range(2):
                    P.op("tensor", lambda e, c0=c0, s=s: e.transpose(
                        p_tb[:, 128 * s:128 * s + 128],
                        bufB[:, 1027 * 32 + c0 + 128 * s:
                             1027 * 32 + c0 + 128 * s + 128], idb[:]),
                        waits=[("CP", cpv)] if s == 0 else [], inc="CPT")
                tv = P.val("CPT")
                P.op("vector", lambda e: e.tensor_copy(btm[:], p_tb[:, 0:256]),
                     waits=[("CPT", tv)], inc="CP")
                gv = P.val("CP")
                def gmm(e, c0=c0):
                    last = None
                    for s in range(2):
                        last = e.matmul(
                            p_tf[:, 256 * s:256 * s + 256],
                            bufB[:, 1027 * 32 + c0 + 128 * s:
                                 1027 * 32 + c0 + 128 * s + 128],
                            bufB[:, 1027 * 33 + c0:1027 * 33 + c0 + CS],
                            start=True, stop=True, skip_group_check=True)
                    return last
                P.op("tensor", gmm, waits=[("CP", gv)], inc="CPT")
                tv = P.val("CPT")
                P.op("vector", lambda e: e.tensor_tensor(
                    Gsb[:], p_tf[:, 0:512], gmask[:], ALU.mult),
                    waits=[("CPT", tv)], inc="CP")
                for s in range(2):
                    for fc8 in range(4):
                        for k8 in range(8):
                            fc = 8 * fc8 + k8
                            w = []
                            if s == 0 and fc8 == 0 and k8 == 0:
                                w.append(("CP", P.val("CP")))
                            if k8 == 0:
                                w.append(("CPH", P.val("CPH")))
                            P.op("tensor", lambda e, c0=c0, s=s, fc=fc, k8=k8:
                                 e.transpose(
                                     p_tb[:, 128 * k8:128 * k8 + 128],
                                     bufB[:, 1027 * fc + c0 + 128 * s:
                                          1027 * fc + c0 + 128 * s + 128],
                                     idb[:]),
                                 waits=w, inc="CPT")
                        tv = P.val("CPT")
                        P.op("vector", lambda e, s=s, fc8=fc8: e.tensor_copy(
                            htm[s][:, 1024 * fc8:1024 * fc8 + 1024],
                            p_tb[:, 0:1024]),
                            waits=[("CPT", tv)], inc="CPH")
                cpv = P.val("CP")
                cphv = P.val("CPH")

                q0 = c * IC * 2

                def emit_rows(q, j2, h2):
                    hh = 2 * j2 + h2
                    w = [("CP", cpv)]
                    if q >= 3:
                        w.append(("RL", q - 2))
                    P.op("sync", lambda e, hh=hh, b=q % 3: e.dma_start(
                        out=rows[b][0:1, :], in_=ac_c[hh:hh + 1, :]),
                        waits=w, inc="R16", n=16)

                def emit_repl(q):
                    w = [("R16", 16 * (q - q0 + 1) + r16base)]
                    if q >= 2:
                        w.append(("EXD", 2 * (q - 1)))
                        w.append(("SDR", q - 1))
                    P.op("tensor", lambda e, b=q % 2, rb=q % 3: e.matmul(
                        p_rep[b][:, 0:256], ones1[:], rows[rb][0:1, :],
                        start=True, stop=True, skip_group_check=True),
                        waits=w, inc="RL")

                r16base = P.val("R16")
                rlbase = P.val("RL")
                emit_rows(q0, 0, 0)
                emit_rows(q0 + 1, 0, 1)
                emit_repl(q0)
                for j in range(IC):
                    pi = c * IC + j
                    for h in range(2):
                        q = 2 * pi + h
                        hh = 2 * j + h
                        par = q % 2
                        if q + 1 < q0 + 2 * IC:
                            if q + 2 < q0 + 2 * IC:
                                j2, h2 = divmod(q + 2 - q0, 2)
                                emit_rows(q + 2, (q + 2 - q0) // 2,
                                          (q + 2 - q0) % 2)
                            emit_repl(q + 1)
                        rl = rlbase + (q - q0) + 1
                        for s in range(2):
                            w = [("RL", rl)] if s == 0 else []
                            if q >= 2 and s == 0:
                                w.append(("MDT", 2 * (q - 1)))
                            P.op("scalar", lambda e, s=s, hh=hh, b=par:
                                 e.activation(
                                     Esb[b][:, 256 * s:256 * s + 256],
                                     p_rep[b][:, 0:256], AF.Exp,
                                     bias=acumT[:, 64 * s + hh:64 * s + hh + 1]),
                                 waits=w, inc="EXD")
                        exd = P.val("EXD")
                        for s in range(2):
                            w = [("EXD", exd)] if s == 0 else []
                            if q >= 1:
                                w.append(("GMD", 2 * (q - 1) + s + 1))
                            P.op("vector", lambda e, s=s, hh=hh, b=par:
                                 e.tensor_scalar(
                                     Mtmp[:, 256 * s:256 * s + 256],
                                     Esb[b][:, 256 * s:256 * s + 256],
                                     1.0,
                                     dtT[:, 64 * s + hh:64 * s + hh + 1],
                                     ALU.min, ALU.mult),
                                 waits=w, inc="MDT")
                            w2 = []
                            if q >= 2 and s == 0:
                                w2.append(("YDD", 2 * (q - 1)))
                            P.op("vector", lambda e, s=s, b=par:
                                 e.tensor_tensor(
                                     MTsb[b][:, 256 * s:256 * s + 256],
                                     Mtmp[:, 256 * s:256 * s + 256],
                                     Gsb[:, 256 * s:256 * s + 256], ALU.mult),
                                 waits=w2, inc="GMD")
                        gmd = P.val("GMD")
                        w = [("CPH", cphv)]
                        if pi >= 1:
                            if h == 0:
                                w.append(("STD", 4 * pi))
                        for s in range(2):
                            seg = 2 * h + s
                            P.op("vector", lambda e, seg=seg, s=s, hh=hh:
                                 e.tensor_scalar_mul(
                                     hdd[:, 64 * seg:64 * seg + 64],
                                     htm[s][:, 64 * hh:64 * hh + 64],
                                     ddT[:, 64 * s + hh:64 * s + hh + 1]),
                                 waits=w if s == 0 else [], inc="HTD")
                        htd = P.val("HTD")
                        w = [("GMD", gmd), ("CPH", cphv)]
                        if pi >= 2 and h == 0:
                            w.append(("U2D", pi - 1))
                        def ydmm(e, h=h, hh=hh, b=par, bp=pi % 2):
                            last = None
                            for s in range(2):
                                last = e.matmul(
                                    p_yd[bp][64 * h:64 * h + 64, 0:256],
                                    htm[s][:, 64 * hh:64 * hh + 64],
                                    MTsb[b][:, 256 * s:256 * s + 256],
                                    start=(s == 0), stop=(s == 1),
                                    skip_group_check=True,
                                    tile_position=(0, 64 * h))
                            return last
                        P.op("tensor", ydmm, waits=w, inc="YDD", n=2)
                    for h in range(2):
                        q = 2 * pi + h
                        w = [("RL", rlbase + (q - q0) + 1)]
                        if pi >= 1 and h == 0:
                            w.append(("U1D", pi))
                        P.op("scalar", lambda e, h=h, b=q % 2: e.activation(
                            sdr[64 * h:64 * h + 64, :],
                            p_rep[b][64 * h:64 * h + 64, 0:256],
                            AF.Exp), waits=w, inc="SDR")
                    sdrv = P.val("SDR")
                    w = [("HTD", P.val("HTD"))]
                    if pi >= 1:
                        w.append(("SEV", pi))
                    def stmm(e):
                        last = None
                        for h in range(2):
                            for s in range(2):
                                seg = 2 * h + s
                                last = e.matmul(
                                    p_st[:, 64 * h:64 * h + 64],
                                    btm[:, 128 * s:128 * s + 128],
                                    hdd[:, 64 * seg:64 * seg + 64],
                                    start=(s == 0), stop=(s == 1),
                                    skip_group_check=True)
                        return last
                    P.op("tensor", stmm, waits=w, inc="STD", n=4)
                    stv = P.val("STD")
                    w = [("STD", stv)]
                    if pi >= 1:
                        w.append(("YOD", 2 * pi))
                    P.op("vector", lambda e: e.tensor_copy(st_sb[:], p_st[:, 0:128]),
                         waits=w, inc="SEV")
                    sev = P.val("SEV")
                    w = [("SEV", sev)]
                    if pi >= 1:
                        w.append(("U1D", pi))
                    def yomm(e, c0=c0):
                        last = None
                        for h in range(2):
                            last = e.matmul(
                                p_yo[64 * h:64 * h + 64, 0:256],
                                st_sb[:, 64 * h:64 * h + 64],
                                bufB[:, 1027 * 33 + c0:1027 * 33 + c0 + CS],
                                start=True, stop=True, skip_group_check=True,
                                tile_position=(0, 64 * h))
                        return last
                    P.op("tensor", yomm, waits=w, inc="YOD", n=2)
                    yod = P.val("YOD")
                    w = [("YOD", yod), ("SDR", sdrv)]
                    if pi >= 1:
                        w.append(("U4D", pi))
                    P.op("vector", lambda e: e.tensor_tensor(
                        u_a[:], p_yo[:, 0:256], sdr[:], ALU.mult),
                        waits=w, inc="U1D")
                    ydd = P.val("YDD")
                    w = [("YDD", ydd)]
                    if pi >= 1:
                        w.append(("U3D", pi))
                    P.op("vector", lambda e, bp=pi % 2: e.tensor_tensor(
                        u_b[:], u_a[:], p_yd[bp][:, 0:256], ALU.add),
                        waits=w, inc="U2D")
                    u2 = P.val("U2D")
                    w = [("U2D", u2)]
                    if pi >= 1:
                        w.append(("U4D", pi))
                    P.op("vector", lambda e, j=j, c0=c0: e.scalar_tensor_tensor(
                        u_a[:], bufB[:, 1027 * j + c0:1027 * j + c0 + CS],
                        dpair[:, j:j + 1], u_b[:], ALU.mult, ALU.add),
                        waits=w, inc="U3D")
                    u3 = P.val("U3D")
                    P.op("gpsimd", lambda e, j=j, c=c, c0=c0: e.tensor_tensor(
                        bufB[:, 1027 * j + c0:1027 * j + c0 + CS],
                        u_a[:], gate_sl(j, CS * c, CS), ALU.mult),
                        waits=[("U3D", u3)], inc="U4D")
                P.barrier()
            if "yf" in debug_taps:
                reg_tap("yf", [128, HBC_C * 1027], BF16)
                P.op("sync", lambda e: e.dma_start(out=taps["yf"], in_=bufB[:]),
                     inc="TAP", n=16)
                P.op("sync", "WAITONLY", waits=[("TAP", P.val("TAP"))])
                P.barrier()

        with ExitStack() as ph:
            def sbp(name, shape, dt=F32):
                return ph.enter_context(nc.sbuf_tensor(name, shape, dt))
            yf2 = sbp("yf2", [128, 1024], BF16)
            row1 = sbp("row1", [1, 512], F32)
            row2 = sbp("row2", [1, 512], F32)
            rrep = sbp("rrep", [128, 1024], F32)
            p_var = ph.enter_context(nc.psum_tensor("p_var", [128, 512], F32))
            p_rr = ph.enter_context(nc.psum_tensor("p_rr", [128, 512], F32))

            for half in range(2):
                for fc in range(IC):
                    w = []
                    if fc >= 2:
                        w.append(("RMM", P.val("RMM") - 1))
                    P.op("scalar", lambda e, fc=fc, half=half: e.activation(
                        yf2[:, 512 * (fc % 2):512 * (fc % 2) + 512],
                        bufB[:, 1027 * fc + HALO + 512 * half:
                             1027 * fc + HALO + 512 * half + 512],
                        AF.Square), waits=w, inc="RSQ")
                    sv = P.val("RSQ")
                    w = [("RSQ", sv)]
                    if half == 1 and fc == 0:
                        w.append(("RRS", 2 * half))
                    P.op("tensor", lambda e, fc=fc: e.matmul(
                        p_var[0:1, :], onesb[:],
                        yf2[:, 512 * (fc % 2):512 * (fc % 2) + 512],
                        start=(fc == 0), stop=(fc == IC - 1),
                        skip_group_check=True),
                        waits=w, inc="RMM")
                mv = P.val("RMM")
                P.op("scalar", lambda e: e.activation(
                    row1[:], p_var[0:1, :], AF.Ln, scale=1.0 / INTER,
                    bias=fcst[0:1, 0:1]),
                    waits=[("RMM", mv)], inc="RRS")
                rr_ = P.val("RRS")
                P.op("scalar", lambda e: e.activation(
                    row2[:], row1[:], AF.Exp, scale=-0.5),
                    waits=[("RRS", rr_)], inc="RRS")
                rv = P.val("RRS")
                w = [("RRS", rv)]
                if half == 1:
                    w.append(("RRC", half))
                P.op("tensor", lambda e: e.matmul(
                    p_rr[:, :], ones1[:], row2[0:1, :], start=True, stop=True,
                    skip_group_check=True),
                    waits=w, inc="RMM")
                mv = P.val("RMM")
                P.op("vector", lambda e, half=half: e.tensor_copy(
                    rrep[:, 512 * half:512 * half + 512], p_rr[:, :]),
                    waits=[("RMM", mv)], inc="RRC")
            rv = P.val("RRC")
            for fc in range(IC):
                P.op("vector", lambda e, fc=fc: e.scalar_tensor_tensor(
                    bufA[:, 1024 * fc:1024 * fc + 1024],
                    bufB[:, 1027 * fc + HALO:1027 * fc + HALO + 1024],
                    normw[:, fc:fc + 1], rrep[:], ALU.mult, ALU.mult),
                    waits=[("RRC", rv)], inc="RYN")
            P.barrier()
            if "yn" in debug_taps:
                reg_tap("yn", [128, GATE_C * 1024], BF16)
                P.op("sync", lambda e: e.dma_start(out=taps["yn"], in_=bufA[:]),
                     inc="TAP", n=16)
                P.op("sync", "WAITONLY", waits=[("TAP", P.val("TAP"))])
                P.barrier()

        with ExitStack() as ph:
            owf = [ph.enter_context(nc.sbuf_tensor(f"owf{i}", [128, 2048], F32))
                  for i in range(2)]
            owb = [ph.enter_context(nc.sbuf_tensor(f"owb{i}", [128, 2048], BF16))
                  for i in range(2)]
            opps = [ph.enter_context(nc.psum_tensor(f"opps{i}", [128, 512], F32))
                   for i in range(4)]
            def wout_fetch(ui):
                hc, kh = divmod(ui, 2)
                b = ui % 2
                w = [("OCA", ui - 1)] if ui >= 2 else []
                P.op("sync", lambda e, hc=hc, kh=kh, b=b: e.dma_start(
                    out=owf[b][:],
                    in_=d_wout[hc, 16 * kh:16 * kh + 16].rearrange(
                        "k p m -> p k m")),
                    waits=w, inc="ODW", n=16)
                w = [("ODW", 16 * (ui + 1))]
                if ui >= 2:
                    w.append(("OMQ", 2 * (ui - 1)))
                P.op("scalar", lambda e, b=b: e.copy(
                    owb[b][:], owf[b][:]), waits=w, inc="OCA")

            ui = 0
            wout_fetch(0)
            for hc in range(DHC):
                for kh in range(2):
                    b = ui % 2
                    if ui + 1 < 2 * DHC:
                        wout_fetch(ui + 1)
                    cav = ui + 1
                    for tt in range(2):
                        w = [("OCA", cav)] if tt == 0 else []
                        if kh == 0 and hc >= 2:
                            w.append(("OEV", 2 * (hc - 1)))
                        slot = (hc % 2) * 2 + tt
                        def omm(e, b=b, kh=kh, tt=tt, slot=slot):
                            last = None
                            for k in range(16):
                                kk = 16 * kh + k
                                last = e.matmul(
                                    opps[slot][:, :],
                                    owb[b][:, 128 * k:128 * k + 128],
                                    bufA[:, 1024 * kk + 512 * tt:
                                         1024 * kk + 512 * tt + 512],
                                    start=(kk == 0), stop=(kk == 31),
                                    skip_group_check=True)
                            return last
                        P.op("tensor", omm, waits=w, inc="OMQ")
                        mqv = P.val("OMQ")
                        if kh == 1:
                            P.op("vector", lambda e, hc=hc, tt=tt, slot=slot:
                                 e.tensor_copy(
                                     bufB[:, 1024 * hc + 512 * tt:
                                          1024 * hc + 512 * tt + 512],
                                     opps[slot][:, :]),
                                 waits=[("OMQ", mqv)], inc="OEV")
                    ui += 1
            P.barrier()
            if "s_attn" in debug_taps:
                reg_tap("s_attn", [128, DHC * 1024], BF16)
                P.op("sync", lambda e: e.dma_start(
                    out=taps["s_attn"], in_=bufB[:, 0:DHC * 1024]), inc="TAP", n=16)
                P.barrier()

        with ExitStack() as ph:
            def sbp(name, shape, dt=F32):
                return ph.enter_context(nc.sbuf_tensor(name, shape, dt))
            w8g = [sbp(f"w8g{i}", [128, 2048], I8) for i in range(2)]
            w8u = [sbp(f"w8u{i}", [128, 2048], I8) for i in range(2)]
            wbg = [sbp(f"wbg{i}", [128, 2048], BF16) for i in range(2)]
            wbu = [sbp(f"wbu{i}", [128, 2048], BF16) for i in range(2)]
            gb = sbp("gb", [128, 512], BF16)
            ub = sbp("ub", [128, 512], BF16)
            p_g = ph.enter_context(nc.psum_tensor("p_g", [128, 512], F32))
            p_u = ph.enter_context(nc.psum_tensor("p_u", [128, 512], F32))
            p_d = [ph.enter_context(nc.psum_tensor(f"p_d{i}", [128, 512], F32))
                   for i in range(2)]

            def ffn_fetch(ui):
                half, ic = divmod(ui, FIC)
                b = ui % 2
                w = []
                if ui >= 2:
                    w.append(("FCA", 2 * ui - 2))
                if half == 1 and ic < 2:
                    w.append(("GCA", 64))
                P.op("sync", lambda e, ic=ic, b=b: e.dma_start(
                    out=w8g[b][:], in_=d_wg[ic].rearrange("k p m -> p k m")),
                    waits=w, inc="FDW", n=16)
                P.op("sync", lambda e, ic=ic, b=b: e.dma_start(
                    out=w8u[b][:], in_=d_wu[ic].rearrange("k p m -> p k m")),
                    inc="FDW", n=16)
                w = [("FDW", 32 * (ui + 1))]
                if ui >= 2:
                    w.append(("FMQ", 2 * (ui - 1)))
                P.op("scalar", lambda e, b=b: e.copy(wbg[b][:], w8g[b][:]),
                     waits=w, inc="FCA")
                P.op("scalar", lambda e, b=b: e.copy(wbu[b][:], w8u[b][:]),
                     inc="FCA")

            for half in range(2):
                for ic in range(FIC):
                    ui = half * FIC + ic
                    b = ui % 2
                    if ic == 0:
                        ffn_fetch(ui)
                    if ic + 1 < FIC:
                        ffn_fetch(ui + 1)
                    cav = 2 * (ui + 1)
                    w = [("FCA", cav)]
                    if ui >= 1:
                        w.append(("FEV", 3 * ui))
                    def gmm2(e, b=b, half=half):
                        last = None
                        for k in range(EC):
                            last = e.matmul(p_g[:, :], wbg[b][:, 128 * k:128 * k + 128],
                                            xn_sl(k, 512 * half, 512),
                                            start=(k == 0), stop=(k == EC - 1))
                        return last
                    P.op("tensor", gmm2, waits=w, inc="FMQ")
                    def umm(e, b=b, half=half):
                        last = None
                        for k in range(EC):
                            last = e.matmul(p_u[:, :], wbu[b][:, 128 * k:128 * k + 128],
                                            xn_sl(k, 512 * half, 512),
                                            start=(k == 0), stop=(k == EC - 1))
                        return last
                    P.op("tensor", umm, inc="FMQ")
                    mqv = P.val("FMQ")
                    w = [("FMQ", mqv - 1)]
                    if ui >= 1:
                        w.append(("FEV", 3 * ui))
                    P.op("scalar", lambda e, ic=ic: e.activation(
                        gb[:], p_g[:, :], AF.Silu, scale=sgate[:, ic:ic + 1]),
                        waits=w, inc="FEV")
                    P.op("vector", lambda e, ic=ic: e.tensor_scalar_mul(
                        ub[:], p_u[:, :], sup[:, ic:ic + 1]),
                        waits=[("FMQ", mqv)], inc="FEV")
                    ev = P.val("FEV")
                    w = [("FEV", ev)]
                    if half == 1 and ic == 0:
                        w.append(("GMQ", 64))
                    P.op("vector", lambda e, ic=ic: e.tensor_tensor(
                        bufA[:, 512 * ic:512 * ic + 512], gb[:], ub[:], ALU.mult),
                        waits=w, inc="FEV")
                fev = P.val("FEV")
                def down_fetch(uu, half=half):
                    dui2, kq = divmod(uu, 4)
                    hc = dui2 - half * DHC
                    b = uu % 2
                    w = [("GCA", uu - 1)] if uu >= 2 else []
                    if kq == 0 and hc == 0:
                        w.append(("FEV", P.val("FEV")))
                    P.op("sync", lambda e, hc=hc, kq=kq, b=b: e.dma_start(
                        out=w8g[b][:],
                        in_=d_wd[hc, 16 * kq:16 * kq + 16].rearrange(
                            "k p m -> p k m")),
                        waits=w, inc="GDW", n=16)
                    w = [("GDW", 16 * (uu + 1))]
                    if uu >= 2:
                        w.append(("GMQ", uu - 1))
                    P.op("scalar", lambda e, b=b: e.copy(
                        wbg[b][:], w8g[b][:]), waits=w, inc="GCA")

                for hc in range(DHC):
                    dui = half * DHC + hc
                    for kq in range(4):
                        uu = 4 * dui + kq
                        b = uu % 2
                        if uu == 4 * half * DHC:
                            down_fetch(uu)
                        if uu + 1 < 4 * (half + 1) * DHC:
                            down_fetch(uu + 1)
                        cav = uu + 1
                        w = [("GCA", cav), ("FEV", fev)]
                        if dui >= 2 and kq == 0:
                            w.append(("GEV", dui - 1))
                        def dmm(e, b=b, kq=kq, dui=dui):
                            last = None
                            for k in range(16):
                                kk = 16 * kq + k
                                last = e.matmul(
                                    p_d[dui % 2][:, :],
                                    wbg[b][:, 128 * k:128 * k + 128],
                                    bufA[:, 512 * kk:512 * kk + 512],
                                    start=(kk == 0), stop=(kk == 63),
                                    skip_group_check=True)
                            return last
                        P.op("tensor", dmm, waits=w, inc="GMQ")
                    mqv = P.val("GMQ")
                    P.op("scalar", lambda e, hc=hc, half=half, dui=dui:
                         e.activation(
                             bufB[:, DHC * 1024 + 1024 * hc + 512 * half:
                                  DHC * 1024 + 1024 * hc + 512 * half + 512],
                             p_d[dui % 2][:, :], AF.Silu,
                             scale=sdown[:, hc:hc + 1]),
                         waits=[("GMQ", mqv)], inc="GEV")
            P.barrier()
            if "g_ffn" in debug_taps:
                reg_tap("g_ffn", [128, DHC * 1024], BF16)
                P.op("sync", lambda e: e.dma_start(
                    out=taps["g_ffn"], in_=bufB[:, DHC * 1024:2 * DHC * 1024]),
                    inc="TAP", n=16)
                P.op("sync", "WAITONLY", waits=[("TAP", P.val("TAP"))])
                P.barrier()

        with ExitStack() as ph:
            def sbp(name, shape, dt=F32):
                return ph.enter_context(nc.sbuf_tensor(name, shape, dt))
            xb = [sbp(f"fxb{i}", [128, 512], F32) for i in range(2)]
            ot = [sbp(f"fot{i}", [128, 512], F32) for i in range(2)]
            sgp = [sbp(f"fsg{i}", [128, 128], BF16) for i in range(2)]
            p_t = [ph.enter_context(nc.psum_tensor(f"fpt{i}", [128, 512], BF16))
                   for i in range(2)]
            un = 0
            for tt in range(8):
                for fq in range(4):
                    b = un % 2
                    w = [("FAD", un - 1)] if un >= 2 else []
                    P.op("sync", lambda e, tt=tt, fq=fq, b=b: e.dma_start(
                        out=xb[b][:], in_=dx[128 * tt:128 * tt + 128,
                                            512 * fq:512 * fq + 512]),
                        waits=w, inc="FXD", n=16)
                    for sub in range(4):
                        fc = 4 * fq + sub
                        w = []
                        if sub >= 1:
                            w.append(("FTR", 4 * un + sub))
                        elif un >= 2:
                            w.append(("FTR", 4 * (un - 1)))
                        P.op("vector", lambda e, fc=fc, tt=tt, b=b: e.tensor_tensor(
                            sgp[b][:], bufB[:, 1024 * fc + 128 * tt:
                                            1024 * fc + 128 * tt + 128],
                            bufB[:, DHC * 1024 + 1024 * fc + 128 * tt:
                                 DHC * 1024 + 1024 * fc + 128 * tt + 128],
                            ALU.mult),
                            waits=w, inc="FSG")
                        sgv = P.val("FSG")
                        w = [("FSG", sgv)]
                        if un >= 2 and sub == 0:
                            w.append(("FAD", un - 1))
                        P.op("tensor", lambda e, sub=sub, b=b: e.transpose(
                            p_t[b][:, 128 * sub:128 * sub + 128], sgp[b][:],
                            idb[:]),
                            waits=w, inc="FTR")
                    trv = P.val("FTR")
                    xdv = P.val("FXD")
                    w = [("FTR", trv), ("FXD", xdv)]
                    if un >= 2:
                        w.append(("FOT", 16 * (un - 1)))
                    P.op("vector", lambda e, b=b: e.tensor_tensor(
                        ot[b][:], p_t[b][:, :], xb[b][:], ALU.add),
                        waits=w, inc="FAD")
                    fav = P.val("FAD")
                    P.op("sync", lambda e, tt=tt, fq=fq, b=b: e.dma_start(
                        out=d_out[128 * tt:128 * tt + 128, 512 * fq:512 * fq + 512],
                        in_=ot[b][:]),
                        waits=[("FAD", fav)], inc="FOT", n=16)
                    un += 1
            P.op("sync", "WAITONLY", waits=[("FOT", P.val("FOT"))])
            if taps:
                P.op("sync", "WAITONLY", waits=[("TAP", P.val("TAP"))])

        with nc.Block() as block:
            P.emit(block)

    return nc, sorted(taps.keys())



def _tile_lhsT(w):
    OUT, IN = w.shape
    OC, KC = OUT // 128, IN // 128
    return np.ascontiguousarray(w.reshape(OC, 128, KC, 128).transpose(0, 2, 3, 1))


def _tile_rhsT(w):
    IN, OUT = w.shape
    KC, OC = IN // 128, OUT // 128
    return np.ascontiguousarray(w.reshape(KC, 128, OC, 128).transpose(2, 0, 1, 3))


def _host_prep(inputs):
    W_in = np.asarray(inputs["L0_attn_W_in"], np.float32)
    W_out = np.asarray(inputs["L0_attn_W_out"], np.float32)
    wg = np.asarray(inputs["L0_ffn_w_gate"], np.int8)
    wu = np.asarray(inputs["L0_ffn_w_up"], np.int8)
    wd = np.asarray(inputs["L0_ffn_w_down"], np.int8)
    convw = np.asarray(inputs["L0_attn_conv_w"], np.float32)
    convb = np.asarray(inputs["L0_attn_conv_b"], np.float32)
    normw = np.asarray(inputs["L0_attn_norm_w"], np.float32)
    s_g = np.asarray(inputs["L0_ffn_s_gate"], np.float32)
    s_u = np.asarray(inputs["L0_ffn_s_up"], np.float32)
    s_d = np.asarray(inputs["L0_ffn_s_down"], np.float32)
    dtbv = np.asarray(inputs["L0_attn_dt_bias"], np.float32)
    alogv = np.asarray(inputs["L0_attn_A_log"], np.float32)
    D = np.asarray(inputs["L0_attn_D"], np.float32)

    W_in_pad = np.zeros((PC * 128, E), np.float32)
    W_in_pad[:PROJ] = W_in
    d = {}
    d["w_in_img"] = _tile_lhsT(W_in_pad)
    d["w_out_img"] = _tile_lhsT(W_out)
    d["wg_img"] = _tile_rhsT(wg)
    d["wu_img"] = _tile_rhsT(wu)
    d["wd_img"] = _tile_rhsT(wd)
    d["convw"] = np.ascontiguousarray(
        convw.reshape(HBC_C, 128, 4).transpose(1, 0, 2).reshape(128, HBC_C * 4))
    d["convb"] = np.ascontiguousarray(convb.reshape(HBC_C, 128).T)
    d["normw"] = np.ascontiguousarray(normw.reshape(IC, 128).T)
    d["sgate"] = np.ascontiguousarray(s_g.reshape(FIC, 128).T)
    d["sup"] = np.ascontiguousarray(s_u.reshape(FIC, 128).T)
    d["sdown"] = np.ascontiguousarray(s_d.reshape(DHC, 128).T)
    d["dtb"] = np.ascontiguousarray(dtbv.reshape(NH, 1))
    d["alog"] = np.ascontiguousarray(alogv.reshape(NH, 1))
    p = np.arange(128)[:, None]
    jj = np.arange(IC)[None, :]
    d["dpair"] = np.ascontiguousarray(D[2 * jj + p // 64].astype(np.float32))
    d["idf"] = np.eye(128, dtype=np.float32)
    d["idb"] = np.eye(128, dtype=np.float32).astype(BF)
    s_i = np.arange(128)[:, None]
    l_i = np.arange(256)[None, :]
    m0 = (s_i <= l_i).astype(np.float32)
    m1 = (s_i + 128 <= l_i).astype(np.float32)
    d["gmask"] = np.ascontiguousarray(np.concatenate([m0, m1], 1).astype(BF))
    fc = np.zeros((128, 2), np.float32); fc[:, 0] = EPS; fc[:, 1] = 1.0
    d["fconst"] = fc
    return d


def _in_maps(inputs, shared=None):
    X = np.asarray(inputs["L0_attn_input_states"], np.float32)
    if shared is None:
        shared = _host_prep(inputs)
    maps = []
    for c in range(8):
        b, h = c // 2, c % 2
        m = dict(shared)
        m["x"] = np.ascontiguousarray(X[b, 1024 * h:1024 * h + 1024])
        if h == 0:
            m["xh"] = np.zeros((HALO, E), np.float32)
        else:
            m["xh"] = np.ascontiguousarray(X[b, 1021:1024])
        maps.append(m)
    return maps


def kernel(**inputs):
    _ = inputs.get("L0_attn_mask")
    nc, _unused = build_nc()
    in_maps = _in_maps(inputs)
    res = run_bass_kernel_spmd(nc, in_maps, core_ids=list(range(8)))
    out = np.empty((4, 2048, E), np.float32)
    for c in range(8):
        b, h = c // 2, c % 2
        out[b, 1024 * h:1024 * h + 1024] = res.results[c]["out"]
    return out
